# version 03 (final)
# baseline (speedup 1.0000x reference)
import numpy as np
from contextlib import ExitStack
import concourse.bass as bass
import concourse.mybir as mybir
from concourse.bass_utils import run_bass_kernel_spmd

F32 = mybir.dt.float32
I32 = mybir.dt.int32
OP = mybir.AluOpType
AF = mybir.ActivationFunctionType
AX = mybir.AxisListType

NCORES = 8
T = 2048
D = 1024
NDC = D // 128
H = 16
DH = 64
C = 64
NCH = T // C
GCOL = 3104
CTRL_OC = 33
FF = 2816
NFC = FF // 128
NXT = T // 128

BW = 128
C_EPS6, C_EPS5, C_HALFPI = 0, 1, 2
NCONST = 3

ENGS = ("sync", "tensor", "vector", "scalar", "gpsimd")


class Builder:
    def __init__(self, dbg=()):
        self.nc = bass.Bass("TRN2", target_bir_lowering=False, debug=False,
                            detect_race_conditions=False)
        self.ctx = ExitStack()
        self.ops = []
        self.dbg = set(dbg)
        self.dbg_outs = {}
        self.sems = {}

    def sem(self, name):
        s = self.ctx.enter_context(self.nc.semaphore(name))
        self.sems[name] = [s, 0]
        return name

    def em(self, eng, fn, inc=None, k=1):
        if inc is None:
            self.ops.append((eng, lambda e, fn=fn: fn(e)))
            return None
        s, _ = self.sems[inc]
        self.sems[inc][1] += k
        tgt = self.sems[inc][1]

        def run(e, fn=fn, s=s, k=k):
            inst = fn(e)
            inst.then_inc(s, k)
        self.ops.append((eng, run))
        return tgt

    def cnt(self, sem):
        return self.sems[sem][1]

    def wait(self, eng, sem, val):
        s, _ = self.sems[sem]
        self.ops.append((eng, lambda e, s=s, v=val: e.wait_ge(s, v)))

    def dma(self, qeng, dst, src, sem):
        return self.em(qeng, lambda e, d=dst, s=src: e.dma_start(out=d, in_=s),
                       inc=sem, k=16)

    def sb(self, name, p, f, dt=F32):
        return self.ctx.enter_context(self.nc.sbuf_tensor(name, [p, f], dt))

    def psum(self, name, p, f):
        return self.ctx.enter_context(self.nc.psum_tensor(name, [p, f], F32))

    def din(self, name, shape, dt=F32):
        return self.nc.dram_tensor(name, shape, dt, kind="ExternalInput").ap()

    def dout(self, name, shape):
        return self.nc.dram_tensor(name, shape, F32, kind="ExternalOutput").ap()

    def dintern(self, name, shape):
        return self.nc.dram_tensor(name, shape, F32, kind="Internal").ap()

    def dbg_store(self, name, shape, fill):
        if name not in self.dbg:
            return
        o = self.nc.dram_tensor("dbg_" + name, shape, F32,
                                kind="ExternalOutput").ap()
        self.dbg_outs[name] = o
        fill(o)

    def finish(self):
        block = self.ctx.enter_context(self.nc.Block())
        ops = self.ops
        with self.ctx:
            @block.sync
            def _(eng):
                for e, fn in ops:
                    if e == "sync":
                        fn(eng)

            @block.tensor
            def _(eng):
                for e, fn in ops:
                    if e == "tensor":
                        fn(eng)

            @block.vector
            def _(eng):
                for e, fn in ops:
                    if e == "vector":
                        fn(eng)

            @block.scalar
            def _(eng):
                for e, fn in ops:
                    if e == "scalar":
                        fn(eng)

            @block.gpsimd
            def _(eng):
                for e, fn in ops:
                    if e == "gpsimd":
                        fn(eng)
        return self.nc


def build(dbg=(), stop=None):
    B = Builder(dbg)
    nc = B.nc

    xg = B.din("xg", [T, D])
    xh = B.din("xh", [3, D])
    wmod_t = B.din("wmod_t", [48, NDC, 128, 128])
    cvec_fm = B.din("cvec_fm", [128, NDC])
    condw_t = B.din("condw_t", [NDC, 128, 128])
    theta_col = B.din("theta_col", [128, 1])
    cnw_col = B.din("cnw_col", [128, 1])
    deltaw_sl = B.din("deltaw_sl", [128, 3200])
    ramp_fm = B.din("ramp_fm", [128, 24])
    bramp_col = B.din("bramp_col", [16, 1])
    ctrlw_t = B.din("ctrlw_t", [CTRL_OC, NDC, 128, 128])
    ctrlb_fm = B.din("ctrlb_fm", [128, CTRL_OC])
    wgate_tm = B.din("wgate_tm", [NDC, 128, D])
    gateb_row = B.din("gateb_row", [1, D])
    convw = B.din("convw", [128, 96])
    outnw_col = B.din("outnw_col", [128, 1])
    wop_t = B.din("wop_t", [NDC, NDC, 128, 128])
    opb_fm = B.din("opb_fm", [128, NDC])
    wg_nat = B.din("wg_nat", [FF, D])
    wu_nat = B.din("wu_nat", [FF, D])
    wd_nat = B.din("wd_nat", [D, FF])
    clip2 = B.din("clip2", [1, 2])
    ident = B.din("ident", [128, 128])
    identA = B.din("identA", [128, 64])
    identB = B.din("identB", [128, 64])
    bdones = B.din("bdones", [128, 128])
    maskSL = B.din("maskSL", [128, 64])
    maskSU = B.din("maskSU", [128, 64])
    maskUI = B.din("maskUI", [128, 64])
    eye2 = B.din("eye2", [128, 64])
    consts = B.din("consts", [128, NCONST])
    jmask = B.din("jmask", [128, 128])
    ones128 = B.din("ones128", [128, 128])
    hmask = B.din("hmask", [128, 1])
    prevmask = B.din("prevmask", [128, 8])
    prevmaskc = B.din("prevmaskc", [128, 8])
    onehot8 = B.din("onehot8", [128, 8])
    expa = B.din("expa", [16, 1024])

    out = B.dout("out", [T, D])

    spill = B.dintern("spill", [8, NCH, 128, 320])
    gate_sp = B.dintern("gate_sp", [NXT, 128, D])
    hffn_sp = B.dintern("hffn_sp", [NFC, 128, T])
    wdq_sp = B.dintern("wdq_sp", [NFC, NDC, 128, 128])
    cc1_in = B.dintern("cc1_in", [1, 128])
    cc1_out = B.dintern("cc1_out", [8, 128])
    cc2_in = B.dintern("cc2_in", [1, H * DH * DH * 2])
    cc2_out = B.dintern("cc2_out", [8, H * DH * DH * 2])
    cc3_in = B.dintern("cc3_in", [1, 128])
    cc3_out = B.dintern("cc3_out", [8, 128])
    cc4_in = B.dintern("cc4_in", [1, 128])
    cc4_out = B.dintern("cc4_out", [8, 128])

    sb = B.sb
    h8 = [sb(f"h8_{i}", 128, 2052) for i in range(NDC)]
    qpo = sb("qpo", 128, T)
    kpo = sb("kpo", 128, T)
    vpo = sb("vpo", 128, T)
    pre = sb("pre", 128, T + 3)
    condb = sb("condb", 128, T)
    gdb = sb("gdb", 32, T)
    g16a = sb("g16a", 16, T)
    g16b = sb("g16b", 16, T)
    ggam = sb("ggam", 16, T)
    gigam = sb("gigam", 16, T)
    wstr = sb("wstr", 128, 2048)
    dwsl = sb("dwsl", 128, 3200)
    msc = sb("msc", 128, 6144)

    M_STATS = 0
    M_MOD = 64
    M_ONEPCA = 112
    M_SMALL = 120
    M_SQ = 256
    M_XTM = 1280
    M_CHUNK = 5376

    ZSC = msc[:, M_XTM:M_XTM + T]
    CHA = msc[:, M_XTM:M_XTM + 3840]
    sb_consts = sb("sb_consts", 128, NCONST)
    sb_ident = sb("sb_ident", 128, 128)
    sb_identA = sb("sb_identA", 128, 64)
    sb_identB = sb("sb_identB", 128, 64)
    sb_bdones = sb("sb_bdones", 128, 128)
    sb_maskSL = sb("sb_maskSL", 128, 64)
    sb_maskSU = sb("sb_maskSU", 128, 64)
    sb_maskUI = sb("sb_maskUI", 128, 64)
    sb_eye2 = sb("sb_eye2", 128, 64)
    sb_theta = sb("sb_theta", 128, 1)
    sb_cnw = sb("sb_cnw", 128, 1)
    sb_ramp = sb("sb_ramp", 128, 24)
    sb_bramp = sb("sb_bramp", 16, 1)
    sb_ctrlb = sb("sb_ctrlb", 128, CTRL_OC)
    sb_convw = sb("sb_convw", 128, 96)
    sb_outnw = sb("sb_outnw", 128, 1)
    sb_opb = sb("sb_opb", 128, NDC)
    sb_clip2 = sb("sb_clip2", 1, 2)
    sb_cvec = sb("sb_cvec", 128, NDC)
    sb_jmask = sb("sb_jmask", 128, 128)
    sb_ones128 = sb("sb_ones128", 128, 128)
    sb_hmask = sb("sb_hmask", 128, 1)
    sb_prevmask = sb("sb_prevmask", 128, 8)
    sb_prevmaskc = sb("sb_prevmaskc", 128, 8)
    sb_onehot8 = sb("sb_onehot8", 128, 8)
    sb_expa2 = sb("sb_expa2", 16, 1024)
    i32b = sb("i32b", 128, 512, I32)
    scrq = sb("scrq", 128, 512)
    tmp3 = sb("tmp3", 64, 768)

    PS = [B.psum(f"ps{i}", 128, 512) for i in range(8)]

    sA = B.sem("sA")
    sT = B.sem("sT")
    sV = B.sem("sV")
    sS = B.sem("sS")
    sG = B.sem("sG")
    sW = B.sem("sW")
    sD = B.sem("sD")
    sWM = [B.sem(f"sWM{i}") for i in range(4)]
    sXS = [B.sem(f"sXS{i}") for i in range(4)]

    em, wait, dma, cnt = B.em, B.wait, B.dma, B.cnt

    for dst, src in [
            (sb_consts[:], consts[:]), (sb_ident[:], ident[:]),
            (sb_identA[:], identA[:]), (sb_identB[:], identB[:]),
            (sb_bdones[:], bdones[:]), (sb_maskSL[:], maskSL[:]),
            (sb_maskSU[:], maskSU[:]), (sb_maskUI[:], maskUI[:]),
            (sb_eye2[:], eye2[:]), (sb_theta[:], theta_col[:]),
            (sb_cnw[:], cnw_col[:]), (sb_ramp[:], ramp_fm[:]),
            (sb_bramp[:], bramp_col[:]), (sb_ctrlb[:], ctrlb_fm[:]),
            (sb_convw[:], convw[:]), (sb_outnw[:], outnw_col[:]),
            (sb_opb[:], opb_fm[:]), (sb_clip2[:], clip2[:]),
            (sb_cvec[:], cvec_fm[:]), (sb_jmask[:], jmask[:]),
            (sb_ones128[:], ones128[:]), (sb_hmask[:], hmask[:]),
            (sb_prevmask[:], prevmask[:]), (sb_prevmaskc[:], prevmaskc[:]),
            (sb_onehot8[:], onehot8[:]),
            (sb_expa2[:], expa[:]), (dwsl[:], deltaw_sl[:])]:
        B.dma("sync", dst, src, sA)
    v_consts = cnt(sA)

    wait("scalar", sA, v_consts)
    NOMOD = "nomod" in B.dbg
    v_siluc = em("scalar", lambda e: e.activation(
        sb_cvec[:], sb_cvec[:], AF.Silu), inc=sS)
    v_statzero = em("vector", lambda e: e.memset(
        msc[:, 0:64], 0.0), inc=sV)

    sb_mod = msc[:, M_MOD:M_MOD + 48]
    sb_onepca = msc[:, M_ONEPCA:M_ONEPCA + 8]
    wmod_buf = [wstr[:, i * 128:(i + 1) * 128] for i in range(4)]

    wait("tensor", sS, v_siluc)
    ev_prev = 0
    base_sT = cnt(sT)
    for oc in range(0 if "nomod" in B.dbg else 48):
        for dc in range(NDC):
            j = oc * NDC + dc
            bi = j % 4
            if j >= 4:
                wait("gpsimd", sT, base_sT + j - 3)
            v = dma("gpsimd", wmod_buf[bi], wmod_t[oc, dc], sWM[bi])
            wait("tensor", sWM[bi], v)
            if dc == 0 and oc >= 2:
                wait("tensor", sV, ev_prev)
            tv = em("tensor", lambda e, oc=oc, bi=bi, dc=dc: e.matmul(
                PS[oc % 2][:, 0:1], wmod_buf[bi], sb_cvec[:, dc:dc + 1],
                start=(dc == 0), stop=(dc == NDC - 1)), inc=sT)
        wait("vector", sT, tv)
        ev = em("vector", lambda e, oc=oc: e.tensor_copy(
            sb_mod[:, oc:oc + 1], PS[oc % 2][:, 0:1]), inc=sV)
        if oc >= 1:
            ev_prev = ev - 1
    v_modevac = cnt(sV)

    scr = msc[:, M_SMALL:M_SMALL + 8]

    def _gfix(e, c0):
        sl = sb_mod[:, c0:c0 + 8]
        e.tensor_single_scalar(scr, sl, 0.0, OP.is_ge)
        e.drain()
        e.tensor_scalar(scr, scr, 2.0, -1.0, OP.mult, OP.add)
        e.drain()
        e.tensor_tensor(sl, sl, scr, OP.mult)
        e.drain()
        e.tensor_scalar(sl, sl, 0.25, None, OP.max)
        e.drain()
        return e.tensor_tensor(sl, sl, scr, OP.mult)

    if not NOMOD:
        em("vector", lambda e: _gfix(e, 16))
        em("vector", lambda e: _gfix(e, 40))
    if NOMOD:
        em("vector", lambda e: e.memset(sb_mod, 0.0))
        v_mod = em("vector", lambda e: e.memset(sb_onepca, 1.0), inc=sV)
    else:
        v_mod = em("vector", lambda e: e.tensor_scalar(
            sb_onepca[:], sb_mod[:, 8:16], 1.0, None, OP.add), inc=sV)

    B.dbg_store("mod", [128, 48], lambda o: (
        wait("gpsimd", sV, v_mod),
        dma("gpsimd", o, sb_mod, sW)))
    B.dbg_store("siluc", [128, NDC], lambda o: (
        wait("gpsimd", sS, v_siluc),
        dma("gpsimd", o, sb_cvec[:], sW)))

    xstage = [msc[:, M_XTM + i * 1024:M_XTM + (i + 1) * 1024] for i in range(4)]
    sqscr = msc[:, M_SQ:M_SQ + 1024]
    stats = msc[:, M_STATS:M_STATS + 64]

    tile_applied = []
    tile_freed_t = []
    for i in range(NXT + 1):
        bi = i % 4
        buf = xstage[bi]
        if i >= 4:
            wait("sync", sT, tile_freed_t[i - 4])
        if i < NXT:
            v = dma("sync", buf, xg[i * 128:(i + 1) * 128, :], sXS[bi])
        else:
            wait("vector", sT, tile_freed_t[i - 4])
            em("vector", lambda e, b=buf: e.memset(b, 0.0))
            vz = em("vector", lambda e: (e.drain(), e.nop())[-1], inc=sV)
            wait("sync", sV, vz)
            v = dma("sync", buf[0:3, :], xh[:], sXS[bi])
        st_m = stats[:, i:i + 1]
        st_v = stats[:, 20 + i:21 + i]
        m2c = stats[:, 40 + i:41 + i]
        wait("vector", sXS[bi], v)
        wait("scalar", sXS[bi], v)
        if i == 0:
            wait("scalar", sV, v_statzero)
        if i == 0 and "xsnap" in B.dbg:
            em("vector", lambda e, b=buf: e.tensor_copy(pre[:, 0:1024], b))
        em("vector", lambda e, b=buf, st_m=st_m: e.reduce_sum(
            st_m, b, axis=AX.X))
        vq = em("scalar", lambda e, b=buf, st_v=st_v: e.activation(
            sqscr, b, AF.Square, accum_out=st_v), inc=sS)
        wait("vector", sS, vq)
        if i == 0 and "xsnap" in B.dbg:
            em("vector", lambda e, st_m=st_m: e.tensor_copy(
                pre[:, 1024:1025], st_m))
            em("vector", lambda e, st_v=st_v: e.tensor_copy(
                pre[:, 1025:1026], st_v))

        def _fin(e, st_m=st_m, st_v=st_v, m2c=m2c):
            e.drain()
            e.tensor_scalar(st_m, st_m, 1.0 / D, None, OP.mult)
            e.drain()
            e.tensor_tensor(m2c, st_m, st_m, OP.mult)
            e.tensor_scalar(st_v, st_v, 1.0 / D, None, OP.mult)
            e.drain()
            return e.tensor_tensor(st_v, st_v, m2c, OP.subtract)
        vv = em("vector", _fin, inc=sV)
        wait("scalar", sV, vv)
        em("scalar", lambda e, st_v=st_v: e.activation(
            st_v, st_v, AF.Ln,
            bias=sb_consts[:, C_EPS5:C_EPS5 + 1]))
        vr = em("scalar", lambda e, st_v=st_v: e.activation(
            st_v, st_v, AF.Exp, scale=-0.5), inc=sS)
        wait("vector", sS, vr)
        va = em("vector", lambda e, b=buf, st_m=st_m, st_v=st_v: e.tensor_scalar(
            b, b, st_m, st_v, OP.subtract, OP.mult), inc=sV)
        if i == 0 and "xsnap" in B.dbg:
            em("vector", lambda e, b=buf: e.tensor_copy(pre[:, 1027:2051], b))
        tile_applied.append(va)

        wait("tensor", sV, va)
        if i == 0:
            wait("tensor", sV, v_mod)
        for b8 in range(NDC):
            pj = PS[2 + ((i * NDC + b8) % 2)]
            if i * NDC + b8 >= 2:
                wait("tensor", sV, ev2ago)
            tv = em("tensor", lambda e, b=buf, b8=b8, p=pj: e.transpose(
                p[:, 0:128], b[:, b8 * 128:(b8 + 1) * 128], sb_ident[:]),
                inc=sT)
            wait("vector", sT, tv)
            if i < NXT:
                dstsl = (h8[b8][:, 3 + i * 128:3 + (i + 1) * 128], slice(None))
                ev = em("vector", lambda e, b8=b8, i=i, p=pj: e.tensor_scalar(
                    h8[b8][:, 3 + i * 128:3 + (i + 1) * 128], p[:, 0:128],
                    sb_onepca[:, b8:b8 + 1], sb_mod[:, b8:b8 + 1],
                    OP.mult, OP.add), inc=sV)
            else:
                ev = em("vector", lambda e, b8=b8, p=pj: e.tensor_scalar(
                    h8[b8][:, 0:3], p[:, 0:3],
                    sb_onepca[:, b8:b8 + 1], sb_mod[:, b8:b8 + 1],
                    OP.mult, OP.add), inc=sV)
            if i * NDC + b8 >= 1:
                ev2ago = ev - 1
        tf = em("tensor", lambda e: (e.drain(), e.nop())[-1], inc=sT)
        tile_freed_t.append(tf)
    v_h_done = cnt(sV)

    def _dump_h(o):
        wait("gpsimd", sV, v_h_done)
        for i in range(NDC):
            dma("gpsimd", o[i * 128:(i + 1) * 128, :], h8[i][:, 0:T + 3], sW)
    B.dbg_store("h", [NDC * 128, T + 3], _dump_h)
    B.dbg_store("xsnap", [128, 2051], lambda o: (
        wait("gpsimd", sV, v_h_done),
        dma("gpsimd", o, pre[:, 0:2051], sW)))
    B.dbg_store("stats", [128, 64], lambda o: (
        wait("gpsimd", sV, v_h_done),
        dma("gpsimd", o, stats, sW)))

    CW = [wstr[:, 1536 + i * 128:1536 + (i + 1) * 128] for i in range(4)]
    sCW = [B.sem(f"sCW{i}") for i in range(4)]
    wait("tensor", sV, v_h_done)
    pevac = []
    for tt in range(4):
        pb = PS[tt % 2]
        for dc in range(NDC):
            j = tt * NDC + dc
            if j >= 4:
                wait("sync", sT, vstart_pmm + j - 3)
            v = dma("sync", CW[j % 4], condw_t[dc], sCW[j % 4])
            wait("tensor", sCW[j % 4], v)
            if tt >= 2 and dc == 0:
                wait("tensor", sV, pevac[tt - 2])
            if j == 0:
                vstart_pmm = cnt(sT)
            em("tensor", lambda e, pb=pb, dc=dc, tt=tt, b=CW[j % 4]: e.matmul(
                pb[:, 0:512], b, h8[dc][:, 3 + tt * 512:3 + (tt + 1) * 512],
                start=(dc == 0), stop=(dc == NDC - 1)), inc=sT)
        wait("vector", sT, cnt(sT))
        ev = em("vector", lambda e, pb=pb, tt=tt: e.tensor_copy(
            ZSC[:, tt * 512:(tt + 1) * 512], pb[:, 0:512]), inc=sV)
        pevac.append(ev)
    MR = M_CHUNK
    PHALO = msc[:, MR:MR + 3]
    ZH = msc[:, MR + 3:MR + 6]
    OCOL = msc[:, MR + 6:MR + 7]
    QCOL = msc[:, MR + 7:MR + 8]
    OUT8 = msc[:, MR + 8:MR + 16]
    ZTS = msc[:, MR + 16:MR + 24]
    ANG = MR + 24
    PANG = MR + 50
    TMPC = msc[:, MR + 54:MR + 56]
    CH_RS = msc[:, MR + 56:MR + 59]
    CONDH = msc[:, MR + 59:MR + 62]
    wait("sync", sT, cnt(sT))
    halo_base = cnt(sT)
    for dc in range(NDC):
        j = dc
        if j >= 4:
            wait("sync", sT, halo_base + j - 3)
        v = dma("sync", CW[j % 4], condw_t[dc], sCW[j % 4])
        wait("tensor", sCW[j % 4], v)
        em("tensor", lambda e, dc=dc, b=CW[j % 4]: e.matmul(
            PS[2][:, 0:3], b, h8[dc][:, 0:3],
            start=(dc == 0), stop=(dc == NDC - 1)), inc=sT)
    wait("vector", sT, cnt(sT))
    v_phalo = em("vector", lambda e: e.tensor_copy(PHALO, PS[2][:, 0:3]),
                 inc=sV)

    if "snapz" in B.dbg:
        em("vector", lambda e: e.tensor_copy(msc[:, 320:576], ZSC[:, 0:256]))
    ccol = msc[:, ANG + 0:ANG + 1]
    scol = msc[:, ANG + 1:ANG + 2]
    wait("scalar", sA, v_consts)
    em("scalar", lambda e: e.activation(
        ccol, sb_theta[:], AF.Sin, bias=sb_consts[:, C_HALFPI:C_HALFPI + 1],
        scale=-1.0))
    v_cs0 = em("scalar", lambda e: e.activation(
        scol, sb_theta[:], AF.Sin), inc=sS)
    wait("vector", sS, v_cs0)

    def _angles(e):
        for j in range(1, 12):
            cP = msc[:, ANG + 2 * (j - 1):ANG + 2 * (j - 1) + 1]
            sP = msc[:, ANG + 2 * (j - 1) + 1:ANG + 2 * (j - 1) + 2]
            cN = msc[:, ANG + 2 * j:ANG + 2 * j + 1]
            sN = msc[:, ANG + 2 * j + 1:ANG + 2 * j + 2]
            e.drain()
            e.tensor_tensor(cN, cP, cP, OP.mult)
            e.tensor_tensor(TMPC[:, 0:1], sP, sP, OP.mult)
            e.tensor_tensor(sN, cP, sP, OP.mult)
            e.drain()
            e.tensor_tensor(cN, cN, TMPC[:, 0:1], OP.subtract)
            e.tensor_scalar(sN, sN, 2.0, None, OP.mult)
        e.drain()
        return e.nop()
    v_ang = em("vector", _angles, inc=sV)

    def _rtile(e, dst, cc, ss, sign):
        e.drain()
        e.tensor_scalar(dst, sb_ident[:], cc, None, OP.mult)
        e.tensor_scalar(TMPR, sb_jmask[:], ss, None, OP.mult)
        e.drain()
        return e.tensor_tensor(dst, dst, TMPR,
                               OP.subtract if sign < 0 else OP.add)
    TMPR = condb[:, 0:128]
    wait("vector", sA, v_consts)
    for j in range(11):
        em("vector", lambda e, j=j: _rtile(
            e, wstr[:, j * 128:(j + 1) * 128],
            msc[:, ANG + 2 * j:ANG + 2 * j + 1],
            msc[:, ANG + 2 * j + 1:ANG + 2 * j + 2], -1))
    RB1 = wstr[:, 11 * 128:12 * 128]
    v_rtiles = em("vector", lambda e: _rtile(e, RB1, ccol, scol, +1), inc=sV)

    v_prev_adds = cnt(sV)
    for j in range(11):
        L = 2 ** j
        segs = []
        a = L
        while a < T:
            b_ = min(a + 512, T)
            segs.append((a, b_))
            a = b_
        wait("tensor", sV, v_rtiles)
        wait("tensor", sV, v_prev_adds)
        mmv = []
        for k, (a, b_) in enumerate(segs):
            em("tensor", lambda e, j=j, a=a, b_=b_, k=k, L=L: e.matmul(
                PS[4 + k][:, 0:b_ - a], wstr[:, j * 128:(j + 1) * 128],
                ZSC[:, a - L:b_ - L], start=True, stop=True), inc=sT)
        wait("vector", sT, cnt(sT))
        for k, (a, b_) in enumerate(segs):
            em("vector", lambda e, a=a, b_=b_, k=k: e.scalar_tensor_tensor(
                ZSC[:, a:b_], PS[4 + k][:, 0:b_ - a], 1.0, ZSC[:, a:b_],
                OP.mult, OP.add), inc=sV)
        v_prev_adds = cnt(sV)
        if "snapz" in B.dbg:
            em("vector", lambda e, j=j: e.tensor_copy(
                msc[:, 576 + j * 16:576 + j * 16 + 16], ZSC[:, 1530:1546]))
    v_scan = cnt(sV)

    wait("gpsimd", sV, v_scan)
    v_zt = dma("gpsimd", cc1_in.rearrange("a b -> b a"), ZSC[:, 2047:2048], sW)
    wait("gpsimd", sW, v_zt)
    v_cc1 = em("gpsimd", lambda e: e.collective_compute(
        "AllGather", OP.bypass, ins=[cc1_in[:]], outs=[cc1_out[:]],
        replica_groups=[[0, 1, 2, 3, 4, 5, 6, 7]]), inc=sG)
    wait("gpsimd", sG, v_cc1)
    for b_ in range(8):
        dma("gpsimd", ZTS[:, b_:b_ + 1],
            cc1_out[b_:b_ + 1, :].rearrange("a b -> b a"), sW)
    v_zts = cnt(sW)

    R2048 = wstr[:, 12 * 128:13 * 128]
    wait("vector", sV, v_ang)
    wait("vector", sW, v_zts)
    em("vector", lambda e: _rtile(
        e, R2048, msc[:, ANG + 22:ANG + 23], msc[:, ANG + 23:ANG + 24], +1))
    v_r2048 = em("vector", lambda e: e.memset(QCOL, 0.0), inc=sV)
    wait("tensor", sV, v_r2048)
    for b_ in range(7, -1, -1):
        vt = em("vector", lambda e, b_=b_: (
            e.drain(),
            e.tensor_scalar(TMPC[:, 0:1], ZTS[:, b_:b_ + 1],
                            sb_prevmask[:, b_:b_ + 1], None, OP.mult),
            e.drain(),
            e.tensor_tensor(TMPC[:, 0:1], TMPC[:, 0:1], QCOL, OP.add))[-1],
            inc=sV)
        wait("tensor", sV, vt)
        tv = em("tensor", lambda e: e.matmul(
            PS[3][:, 0:1], R2048, TMPC[:, 0:1], start=True, stop=True),
            inc=sT)
        wait("vector", sT, tv)
        em("vector", lambda e: e.tensor_copy(QCOL, PS[3][:, 0:1]), inc=sV)
    pc = msc[:, PANG:PANG + 1]
    ps_ = msc[:, PANG + 1:PANG + 2]
    em("vector", lambda e: (e.memset(pc, 1.0), e.memset(ps_, 0.0))[-1])
    em("vector", lambda e: e.drain())
    em("vector", lambda e: e.tensor_copy(OUT8[:, 0:1], QCOL))
    for k in range(1, 8):
        def _angmul(e):
            c2 = msc[:, ANG + 22:ANG + 23]
            s2 = msc[:, ANG + 23:ANG + 24]
            e.drain()
            e.tensor_tensor(TMPC[:, 0:1], pc, c2, OP.mult)
            e.tensor_tensor(TMPC[:, 1:2], ps_, s2, OP.mult)
            e.tensor_tensor(ps_, ps_, c2, OP.mult)
            e.tensor_tensor(pc, pc, s2, OP.mult)
            e.drain()
            e.tensor_tensor(ps_, ps_, pc, OP.add)
            e.drain()
            e.tensor_tensor(pc, TMPC[:, 0:1], TMPC[:, 1:2], OP.subtract)
            e.drain()
            return e.nop()
        em("vector", _angmul)
        RK = wstr[:, 13 * 128:14 * 128]
        vb = em("vector", lambda e: _rtile(e, RK, pc, ps_, -1), inc=sV)
        wait("tensor", sV, vb)
        tv = em("tensor", lambda e: e.matmul(
            PS[3][:, 4:5], RK, QCOL, start=True, stop=True), inc=sT)
        wait("vector", sT, tv)
        em("vector", lambda e, k=k: e.tensor_copy(OUT8[:, k:k + 1],
                                                  PS[3][:, 4:5]), inc=sV)
    v_osel = em("vector", lambda e: (
        e.drain(),
        e.tensor_tensor(OUT8[:], OUT8[:], sb_onehot8[:], OP.mult),
        e.drain(),
        e.reduce_sum(OCOL, OUT8[:], axis=AX.X))[-1], inc=sV)

    wait("tensor", sV, v_osel)
    tv = em("tensor", lambda e: e.matmul(
        PS[3][:, 8:9], wstr[:, 0:128], OCOL, start=True, stop=True), inc=sT)
    wait("vector", sT, tv)
    v_w0 = em("vector", lambda e: e.tensor_copy(pre[:, 0:1], PS[3][:, 8:9]),
              inc=sV)
    v_prevw = v_w0
    for j in range(11):
        L = 2 ** j
        Lp = min(L, T - L)
        wait("tensor", sV, v_prevw)
        a = 0
        mm_ct = 0
        while a < Lp:
            b_ = min(a + 512, Lp)
            em("tensor", lambda e, j=j, a=a, b_=b_, k=mm_ct: e.matmul(
                PS[4 + (k % 4)][:, 0:b_ - a], wstr[:, j * 128:(j + 1) * 128],
                pre[:, a:b_], start=True, stop=True), inc=sT)
            a = b_
            mm_ct += 1
        wait("vector", sT, cnt(sT))
        a = 0
        mm_ct = 0
        while a < Lp:
            b_ = min(a + 512, Lp)
            em("vector", lambda e, L=L, a=a, b_=b_, k=mm_ct: e.tensor_copy(
                pre[:, L + a:L + b_], PS[4 + (k % 4)][:, 0:b_ - a]), inc=sV)
            a = b_
            mm_ct += 1
        v_prevw = cnt(sV)
    v_wgen_mm = cnt(sT)
    wait("vector", sV, v_prevw)
    v_condb = em("vector", lambda e: (
        e.drain(),
        e.tensor_tensor(condb[:, 0:T], ZSC[:, 0:T], pre[:, 0:T], OP.add))[-1],
        inc=sV)
    wait("scalar", sV, v_condb)
    v_sq = em("scalar", lambda e: e.activation(
        qpo[:, 0:T], condb[:, 0:T], AF.Square), inc=sS)
    wait("tensor", sS, v_sq)
    for tt in range(4):
        em("tensor", lambda e, tt=tt: e.matmul(
            PS[4 + tt][:, 0:512], sb_ones128[:],
            qpo[:, tt * 512:(tt + 1) * 512], start=True, stop=True), inc=sT)
    v_rms_mm = cnt(sT)
    wait("scalar", sT, cnt(sT))
    for tt in range(4):
        em("scalar", lambda e, tt=tt: e.activation(
            kpo[:, tt * 512:(tt + 1) * 512], PS[4 + tt][:, 0:512], AF.Ln,
            bias=sb_consts[:, C_EPS6:C_EPS6 + 1], scale=1.0 / BW))
    v_lnx = em("scalar", lambda e: (
        e.activation(kpo[:, 0:T], kpo[:, 0:T], AF.Exp, scale=-0.5)),
        inc=sS)
    wait("vector", sS, v_lnx)
    v_cond = em("vector", lambda e: (
        e.drain(),
        e.tensor_tensor(condb[:, 0:T], condb[:, 0:T], kpo[:, 0:T], OP.mult),
        e.drain(),
        e.tensor_scalar(condb[:, 0:T], condb[:, 0:T], sb_cnw[:, 0:1], None,
                        OP.mult))[-1], inc=sV)

    wait("vector", sV, v_phalo)
    vbs = em("vector", lambda e: (
        e.drain(), e.tensor_copy(ZH[:, 2:3], OCOL),
        e.drain(),
        e.tensor_tensor(TMPC[:, 0:1], OCOL, PHALO[:, 2:3], OP.subtract))[-1],
        inc=sV)
    wait("tensor", sV, vbs)
    tv = em("tensor", lambda e: e.matmul(
        PS[3][:, 12:13], RB1, TMPC[:, 0:1], start=True, stop=True), inc=sT)
    wait("vector", sT, tv)
    em("vector", lambda e: (
        e.tensor_copy(ZH[:, 1:2], PS[3][:, 12:13]),
        e.drain(),
        e.tensor_tensor(TMPC[:, 0:1], ZH[:, 1:2], PHALO[:, 1:2],
                        OP.subtract))[-1], inc=sV)
    wait("tensor", sV, cnt(sV))
    tv = em("tensor", lambda e: e.matmul(
        PS[3][:, 16:17], RB1, TMPC[:, 0:1], start=True, stop=True), inc=sT)
    wait("vector", sT, tv)
    v_zh = em("vector", lambda e: e.tensor_copy(ZH[:, 0:1], PS[3][:, 16:17]),
              inc=sV)
    wait("scalar", sV, v_zh)
    v_zhsq = em("scalar", lambda e: e.activation(
        CONDH, ZH[:], AF.Square), inc=sS)
    wait("tensor", sS, v_zhsq)
    tv = em("tensor", lambda e: e.matmul(
        PS[3][:, 20:23], sb_ones128[:], CONDH, start=True, stop=True),
        inc=sT)
    wait("scalar", sT, tv)
    em("scalar", lambda e: e.activation(
        CH_RS, PS[3][:, 20:23], AF.Ln,
        bias=sb_consts[:, C_EPS6:C_EPS6 + 1], scale=1.0 / BW))
    v_chr = em("scalar", lambda e: e.activation(
        CH_RS, CH_RS, AF.Exp, scale=-0.5), inc=sS)
    wait("vector", sS, v_chr)
    v_condh = em("vector", lambda e: (
        e.drain(),
        e.tensor_tensor(CONDH, ZH[:], CH_RS, OP.mult),
        e.drain(),
        e.tensor_scalar(CONDH, CONDH, sb_cnw[:, 0:1], None, OP.mult))[-1],
        inc=sV)

    B.dbg_store("qcol", [128, 1], lambda o: (
        wait("gpsimd", sV, v_osel),
        dma("gpsimd", o, QCOL, sW)))
    B.dbg_store("out8", [128, 8], lambda o: (
        wait("gpsimd", sV, v_osel),
        dma("gpsimd", o, OUT8[:], sW)))
    B.dbg_store("ocol", [128, 1], lambda o: (
        wait("gpsimd", sV, v_osel),
        dma("gpsimd", o, OCOL, sW)))
    B.dbg_store("zts", [128, 8], lambda o: (
        wait("gpsimd", sW, v_zts),
        dma("gpsimd", o, ZTS[:], sW)))
    if "snapcond" in B.dbg:
        em("vector", lambda e: e.tensor_copy(
            msc[:, M_XTM + 2048:M_XTM + 2304], condb[:, 0:256]))
        B.dbg_store("snapcond", [128, 256], lambda o: (
            wait("gpsimd", sV, v_cond),
            dma("gpsimd", o, msc[:, M_XTM + 2048:M_XTM + 2304], sW)))
    B.dbg_store("cond", [128, T], lambda o: (
        wait("gpsimd", sV, v_cond),
        dma("gpsimd", o, condb[:, 0:T], sW)))
    B.dbg_store("condh", [128, 3], lambda o: (
        wait("gpsimd", sV, v_condh),
        dma("gpsimd", o, CONDH, sW)))
    B.dbg_store("snapz", [128, 768], lambda o: (
        wait("gpsimd", sV, v_scan),
        dma("gpsimd", o, msc[:, 320:1088], sW)))
    B.dbg_store("zscan", [128, T], lambda o: (
        wait("gpsimd", sV, v_scan),
        dma("gpsimd", o, ZSC[:, 0:T], sW)))


    def emit_chunks(hp):
        SL = {"P1": 0, "P2": 64, "P4": 128, "P8": 192, "P16": 256,
              "Q1": 320, "Q2": 384, "Q4": 448, "Q8": 512, "Q16": 576,
              "Q32": 640, "XT": 704, "MT": 832, "KT": 896}
        GTo, HMo, QHo, O0o, W2o = 0, 64, 128, 192, 448

        def slot(g, name, w=64):
            o = g * 960 + (SL[name] if isinstance(name, str) else name)
            return CHA[:, o:o + w]

        def halves(ap):
            return (ap[0:64, :], ap[64:128, :])

        wait("vector", sW, cnt(sW))
        vi = em("vector", lambda e: (
            e.drain(),
            e.tensor_copy(GACC, sb_eye2[:]),
            e.memset(HACC, 0.0))[-1], inc=sV)

        prev_spill = cnt(sSP)
        for g0 in range(0, NCH, 4):
            gn = min(4, NCH - g0)
            cis = list(range(g0, g0 + gn))

            def tok(ci):
                return slice(64 * ci, 64 * ci + 64)

            if stop == "ck0" and hp == 0 and g0 == 0:
                return cnt(sV)
            wait("tensor", sV, cnt(sV))
            for g, ci in enumerate(cis):
                for lo, hi in ((0, 64), (64, 128)):
                    em("tensor", lambda e, g=g, ci=ci, lo=lo, hi=hi: e.matmul(
                        PS[g][lo:hi, 0:64], pre[lo:hi, tok(ci)],
                        kpo[lo:hi, tok(ci)], start=True, stop=True))
                    em("tensor", lambda e, g=g, ci=ci, lo=lo, hi=hi: e.matmul(
                        PS[g][lo:hi, 64:128], kpo[lo:hi, tok(ci)],
                        pre[lo:hi, tok(ci)], start=True, stop=True))
            if stop == "ck1" and hp == 0 and g0 == 0:
                tph = em("tensor", lambda e: (e.drain(), e.nop())[-1],
                         inc=sT)
                return cnt(sV)
            tph = em("tensor", lambda e: (e.drain(), e.nop())[-1], inc=sT)
            wait("vector", sT, tph)
            wait("vector", sSP, prev_spill)
            em("vector", lambda e: e.drain())
            for g, ci in enumerate(cis):
                if stop == "ckA" and hp == 0 and g0 == 0 and g > 0:
                    continue
                em("vector", lambda e, g=g: e.tensor_tensor(
                    slot(g, "P1"), PS[g][:, 0:64], sb_maskSL[:], OP.mult))
                if stop in ("ckB", "ckA") and hp == 0 and g0 == 0:
                    continue
                em("vector", lambda e, g=g: e.tensor_tensor(
                    slot(g, "Q1"), PS[g][:, 64:128], sb_maskSU[:], OP.mult))
            if stop == "ck2" and hp == 0 and g0 == 0:
                return cnt(sV)
            vph = em("vector", lambda e: (e.drain(), e.nop())[-1], inc=sV)
            wait("tensor", sV, vph)
            for g, ci in enumerate(cis):
                for k3, (src_t, off) in enumerate(
                        ((vpo, 128), (pre, 192), (kpo, 256))):
                    em("tensor", lambda e, g=g, ci=ci, s=src_t, off=off: (
                        e.transpose(PS[g][0:64, off:off + 64],
                                    s[0:64, tok(ci)], sb_identA[0:64, :]),
                        e.drain())[-1])
                    em("tensor", lambda e, g=g, ci=ci, s=src_t, k3=k3: (
                        e.transpose(
                            PS[g][0:64, 320 + k3 * 64:384 + k3 * 64],
                            s[64:128, tok(ci)], sb_identB[64:128, :]),
                        e.drain())[-1])
            tphB = em("tensor", lambda e: (e.drain(), e.nop())[-1], inc=sT)
            if stop == "ck3a" and hp == 0 and g0 == 0:
                return cnt(sV)
            wait("vector", sT, tphB)
            em("vector", lambda e: e.drain())
            for g, ci in enumerate(cis):
                em("vector", lambda e, g=g: e.tensor_copy(
                    tmp3[0:64, g * 192:g * 192 + 192],
                    PS[g][0:64, 320:512]))
            vevB = em("vector", lambda e: (e.drain(), e.nop())[-1], inc=sV)
            if stop == "ck3b" and hp == 0 and g0 == 0:
                return cnt(sV)
            wait("tensor", sV, vevB)
            for g, ci in enumerate(cis):
                for k3 in range(3):
                    em("tensor", lambda e, g=g, k3=k3: e.matmul(
                        PS[g][64:128, 128 + k3 * 64:192 + k3 * 64],
                        sb_identA[0:64, :],
                        tmp3[0:64, g * 192 + k3 * 64:g * 192 + k3 * 64 + 64],
                        start=True, stop=True))
            tph = em("tensor", lambda e: (e.drain(), e.nop())[-1], inc=sT)
            if stop == "ck3c" and hp == 0 and g0 == 0:
                return cnt(sV)
            wait("vector", sT, tph)
            em("vector", lambda e: e.drain())
            for g, ci in enumerate(cis):
                em("vector", lambda e, g=g: e.tensor_copy(
                    slot(g, "XT", 64), PS[g][:, 128:192]))
                em("vector", lambda e, g=g: e.tensor_copy(
                    slot(g, "XT", 128)[:, 64:128], PS[g][:, 192:256]))
                em("vector", lambda e, g=g, ci=ci: e.tensor_scalar(
                    slot(g, "KT"), PS[g][:, 256:320],
                    GCREP[:, ci:ci + 1], None, OP.mult))
            vph = em("vector", lambda e: (e.drain(), e.nop())[-1], inc=sV)
            if hp == 0 and g0 == 0 and "xprog" in B.dbg:
                B.dbg_store("xprog", [8, 128, 128], lambda o: None)
                xpo = B.dbg_outs["xprog"]
                wait("gpsimd", sV, vph)
                dma("gpsimd", xpo[0, :, 0:64], vpo[:, 0:64], sW)
                dma("gpsimd", xpo[1], slot(0, "XT", 128), sW)
                wait("vector", sW, cnt(sW))
            if stop == "ck4" and hp == 0 and g0 == 0:
                return cnt(sV)
            prevP = {0: "P1"}
            prevQ = {0: "Q1"}
            names = [("P2", "Q2"), ("P4", "Q4"), ("P8", "Q8"),
                     ("P16", "Q16"), (None, "Q32")]
            pn, qn = "P1", "Q1"
            for lv, (pname, qname) in enumerate(names):
                wait("tensor", sV, vph)
                for g, ci in enumerate(cis):
                    for lo, hi in ((0, 64), (64, 128)):
                        if pname is not None:
                            em("tensor", lambda e, g=g, lo=lo, hi=hi, pn=pn,
                               qn=qn: e.matmul(
                                   PS[g][lo:hi, 0:64], slot(g, qn)[lo:hi, :],
                                   slot(g, pn)[lo:hi, :],
                                   start=True, stop=True))
                        em("tensor", lambda e, g=g, lo=lo, hi=hi, pn=pn,
                           qn=qn: e.matmul(
                               PS[g][lo:hi, 64:128], slot(g, pn)[lo:hi, :],
                               slot(g, qn)[lo:hi, :], start=True, stop=True))
                tph = em("tensor", lambda e: (e.drain(), e.nop())[-1], inc=sT)
                wait("vector", sT, tph)
                em("vector", lambda e: e.drain())
                for g, ci in enumerate(cis):
                    if pname is not None:
                        em("vector", lambda e, g=g, pname=pname: e.tensor_copy(
                            slot(g, pname), PS[g][:, 0:64]))
                    em("vector", lambda e, g=g, qname=qname: e.tensor_copy(
                        slot(g, qname), PS[g][:, 64:128]))
                vph = em("vector", lambda e: (e.drain(), e.nop())[-1], inc=sV)
                pn, qn = pname, qname
            if stop == "ck5" and hp == 0 and g0 == 0:
                return cnt(sV)
            for fidx, (f, sgn) in enumerate((
                    ("Q32", 1.0), ("Q16", 1.0), ("Q8", 1.0),
                    ("Q4", 1.0), ("Q2", 1.0), ("Q1", -1.0))):
                wait("tensor", sV, vph)
                for g, ci in enumerate(cis):
                    for lo, hi in ((0, 64), (64, 128)):
                        em("tensor", lambda e, g=g, lo=lo, hi=hi, f=f:
                           e.matmul(PS[g][lo:hi, 128:256],
                                    slot(g, f)[lo:hi, :],
                                    slot(g, "XT", 128)[lo:hi, :],
                                    start=True, stop=True))
                tph = em("tensor", lambda e: (e.drain(), e.nop())[-1], inc=sT)
                wait("vector", sT, tph)
                em("vector", lambda e: e.drain())
                for g, ci in enumerate(cis):
                    em("vector", lambda e, g=g, sgn=sgn:
                       e.scalar_tensor_tensor(
                           slot(g, "XT", 128), PS[g][:, 128:256], sgn,
                           slot(g, "XT", 128), OP.mult, OP.add))
                vph = em("vector", lambda e: (e.drain(), e.nop())[-1], inc=sV)
                if hp == 0 and g0 == 0 and "xprog" in B.dbg:
                    xpo = B.dbg_outs["xprog"]
                    wait("gpsimd", sV, vph)
                    dma("gpsimd", xpo[2 + fidx], slot(0, "XT", 128), sW)
                    wait("vector", sW, cnt(sW))
            if stop == "ck6" and hp == 0 and g0 == 0:
                return cnt(sV)
            wait("tensor", sV, vph)
            for g, ci in enumerate(cis):
                for lo, hi in ((0, 64), (64, 128)):
                    em("tensor", lambda e, g=g, ci=ci, lo=lo, hi=hi: e.matmul(
                        PS[g][lo:hi, 0:64], kpo[lo:hi, tok(ci)],
                        qpo[lo:hi, tok(ci)], start=True, stop=True))
            tph = em("tensor", lambda e: (e.drain(), e.nop())[-1], inc=sT)
            wait("vector", sT, tph)
            em("vector", lambda e: e.drain())
            for g, ci in enumerate(cis):
                em("vector", lambda e, g=g: e.tensor_tensor(
                    slot(g, "MT"), PS[g][:, 0:64], sb_maskUI[:], OP.mult))
            vph = em("vector", lambda e: (e.drain(), e.nop())[-1], inc=sV)
            em("vector", lambda e: e.drain())
            for g, ci in enumerate(cis):
                em("vector", lambda e, g=g: e.memset(
                    CHA[:, g * 960 + W2o:g * 960 + W2o + 128], 0.0))
                em("vector", lambda e, g=g: e.memset(
                    CHA[64:128, g * 960 + O0o:g * 960 + O0o + 128], 0.0))
            em("vector", lambda e: e.drain())
            for g, ci in enumerate(cis):
                em("vector", lambda e, g=g: e.tensor_copy(
                    CHA[0:64, g * 960 + W2o:g * 960 + W2o + 64],
                    slot(g, "XT", 128)[0:64, 0:64]))
                em("vector", lambda e, g=g: e.tensor_copy(
                    CHA[64:128, g * 960 + W2o + 64:g * 960 + W2o + 128],
                    slot(g, "XT", 128)[64:128, 0:64]))
            vph = em("vector", lambda e: (e.drain(), e.nop())[-1], inc=sV)
            if stop == "ck7" and hp == 0 and g0 == 0:
                return cnt(sV)
            wait("tensor", sV, vph)
            if hp == 0 and g0 == 0 and "ph8in" in B.dbg:
                B.dbg_store("ph8in", [3, 128, 128], lambda o: None)
                p8o = B.dbg_outs["ph8in"]
                wait("gpsimd", sV, vph)
                dma("gpsimd", p8o[0], slot(0, "XT", 128), sW)
                dma("gpsimd", p8o[1, :, 0:64], slot(0, "KT"), sW)
                dma("gpsimd", p8o[2, :, 0:64], slot(0, "MT"), sW)
                wait("tensor", sW, cnt(sW))
            for g, ci in enumerate(cis):
                for lo, hi in ((0, 64), (64, 128)):
                    em("tensor", lambda e, g=g, lo=lo, hi=hi: e.matmul(
                        PS[g][lo:hi, 256:320],
                        slot(g, "XT", 128)[lo:hi, 64:128],
                        slot(g, "KT")[lo:hi, :], start=True, stop=True))
                    em("tensor", lambda e, g=g, lo=lo, hi=hi: e.matmul(
                        PS[g][lo:hi, 320:384], slot(g, "KT")[lo:hi, :],
                        slot(g, "XT", 128)[lo:hi, 0:64],
                        start=True, stop=True))
                    em("tensor", lambda e, g=g, lo=lo, hi=hi: e.matmul(
                        PS[g][lo:hi, 128:192],
                        slot(g, "XT", 128)[lo:hi, 64:128],
                        slot(g, "MT")[lo:hi, :], start=True, stop=True))
                em("tensor", lambda e, g=g: (e.matmul(
                    PS[g][0:64, 384:512], slot(g, "MT")[0:64, :],
                    CHA[0:64, g * 960 + W2o:g * 960 + W2o + 128],
                    start=True, stop=False), e.drain())[-1])
                em("tensor", lambda e, g=g: (e.matmul(
                    PS[g][0:64, 384:512], slot(g, "MT")[64:128, :],
                    CHA[64:128, g * 960 + W2o:g * 960 + W2o + 128],
                    start=False, stop=True), e.drain())[-1])
            tph = em("tensor", lambda e: (e.drain(), e.nop())[-1], inc=sT)
            wait("vector", sT, tph)
            em("vector", lambda e: e.drain())
            for g, ci in enumerate(cis):
                em("vector", lambda e, g=g, ci=ci: e.tensor_scalar(
                    slot(g, "Q2"), sb_eye2[:], GCREP[:, ci:ci + 1], None,
                    OP.mult))
                em("vector", lambda e, g=g: e.drain())
                em("vector", lambda e, g=g: e.scalar_tensor_tensor(
                    slot(g, GTo), PS[g][:, 256:320], -1.0,
                    slot(g, "Q2"), OP.mult, OP.add))
                em("vector", lambda e, g=g: e.tensor_copy(
                    slot(g, HMo), PS[g][:, 320:384]))
                em("vector", lambda e, g=g: e.tensor_copy(
                    CHA[0:64, g * 960 + O0o:g * 960 + O0o + 128],
                    PS[g][0:64, 384:512]))
                em("vector", lambda e, g=g, ci=ci: e.scalar_tensor_tensor(
                    slot(g, QHo), PS[g][:, 128:192], -1.0,
                    qpo[:, tok(ci)], OP.mult, OP.add))
            vph = em("vector", lambda e: (e.drain(), e.nop())[-1], inc=sV)
            if stop == "ck8" and hp == 0 and g0 == 0:
                return cnt(sV)
            wait("gpsimd", sV, vph)
            if hp == 0 and g0 == 0 and "hch" in B.dbg:
                B.dbg_store("hch", [4, 3, 128, 64], lambda o: None)
                hco9 = B.dbg_outs["hch"]
                dma("gpsimd", hco9[0, 2], slot(0, HMo), sW)
                dma("gpsimd", hco9[1, 2], slot(1, HMo), sW)
                wait("vector", sW, cnt(sW))
            for g, ci in enumerate(cis):
                dma("gpsimd", spill[hp, ci],
                    CHA[:, g * 960:g * 960 + 320], sSP)
            prev_spill = cnt(sSP)
            if stop == "ck9" and hp == 0 and g0 == 0:
                return cnt(sV)
            for g, ci in enumerate(cis):
                wait("tensor", sV, cnt(sV))
                for lo, hi in ((0, 64), (64, 128)):
                    em("tensor", lambda e, g=g, lo=lo, hi=hi: e.matmul(
                        PS[g][lo:hi, 0:64], slot(g, GTo)[lo:hi, :],
                        GACC[lo:hi, :], start=True, stop=True))
                    em("tensor", lambda e, g=g, lo=lo, hi=hi: e.matmul(
                        PS[g][lo:hi, 64:128], slot(g, GTo)[lo:hi, :],
                        HACC[lo:hi, :], start=True, stop=True))
                tcm = em("tensor", lambda e: (e.drain(), e.nop())[-1], inc=sT)
                wait("vector", sT, tcm)
                em("vector", lambda e: e.drain())
                em("vector", lambda e, g=g: e.tensor_copy(
                    GACC, PS[g][:, 0:64]))
                em("vector", lambda e, g=g: e.scalar_tensor_tensor(
                    HACC, PS[g][:, 64:128], 1.0, slot(g, HMo),
                    OP.mult, OP.add), inc=sV)
            if hp == 0 and "hprog" in B.dbg:
                gi = g0 // 4
                if gi == 0:
                    B.dbg_store("hprog", [8, 128, 64], lambda o: None)
                hpo = B.dbg_outs["hprog"]
                wait("gpsimd", sV, cnt(sV))
                dma("gpsimd", hpo[gi], HACC, sW)
                wait("vector", sW, cnt(sW))
            if hp == 0 and g0 == 0 and "hch" in B.dbg:
                hco = B.dbg_outs["hch"]
                wait("gpsimd", sV, cnt(sV))
                for g, ci in enumerate(cis):
                    dma("gpsimd", hco[g, 0], slot(g, GTo), sW)
                    dma("gpsimd", hco[g, 1], slot(g, HMo), sW)
                wait("vector", sW, cnt(sW))
        v_hpmaps = cnt(sV)
        return v_hpmaps

    GACC = msc[:, 0:64]
    HACC = msc[:, 256:320]
    GCREP = msc[:, 120:152]
    sCT = [B.sem(f"sCT{i}") for i in range(2)]
    sSP = B.sem("sSP")

    v_qkv_dbg = {}
    if stop != "condonly":
        wait("scalar", sA, v_consts)
        em("scalar", lambda e: e.activation(sb_ramp[:], sb_ramp[:], AF.Sigmoid))
        v_sbr = em("scalar", lambda e: e.activation(
            sb_bramp[:], sb_bramp[:], AF.Sigmoid), inc=sS)

        wait("sync", sT, v_wgen_mm)
        for dc in range(NDC):
            v24 = dma("sync", wstr[:, dc * 128:(dc + 1) * 128],
                      ctrlw_t[24, dc], sCT[0])
        wait("tensor", sCT[0], v24)
        wait("tensor", sV, v_h_done)
        wait("tensor", sS, v_chr)
        gdb_evs = {}
        for tt in range(4):
            pbank = PS[5] if tt % 2 == 0 else PS[6]
            if tt >= 2:
                wait("tensor", sV, gdb_evs[tt - 2])
            for dc in range(NDC):
                em("tensor", lambda e, pbank=pbank, tt=tt, dc=dc: e.matmul(
                    pbank[:, 0:512], wstr[:, dc * 128:(dc + 1) * 128],
                    h8[dc][:, 3 + tt * 512:3 + tt * 512 + 512],
                    start=(dc == 0), stop=(dc == NDC - 1)),
                    inc=sT if dc == NDC - 1 else None)
            wait("vector", sT, cnt(sT))
            gdb_evs[tt] = em("vector", lambda e, pbank=pbank, tt=tt:
                             e.tensor_scalar(
                                 gdb[:, tt * 512:(tt + 1) * 512],
                                 pbank[0:32, 0:512],
                                 sb_ctrlb[0:32, 24:25], None, OP.add),
                             inc=sV)
        v_c0mm = cnt(sT)
        v_gdb = cnt(sV)
        if "snapc0" in B.dbg:
            em("vector", lambda e: e.tensor_copy(
                msc[0:32, 320:576], gdb[:, 0:256]))
        wait("scalar", sV, v_gdb)
        em("scalar", lambda e: e.activation(g16a[:], gdb[0:16, :], AF.Sigmoid))
        v_lng = em("scalar", lambda e: e.activation(g16a[:], g16a[:], AF.Ln),
                   inc=sS)
        wait("vector", sS, v_lng)
        if "snapc0" in B.dbg:
            em("vector", lambda e: e.tensor_copy(
                msc[0:16, 576:832], g16a[:, 0:256]))
        for ci in range(NCH):
            em("vector", lambda e, ci=ci: e.tensor_tensor_scan(
                g16b[:, ci * C:(ci + 1) * C], g16a[:, ci * C:(ci + 1) * C],
                g16a[:, ci * C:(ci + 1) * C], 0.0, OP.add, OP.bypass),
                inc=sV if ci == NCH - 1 else None)
        v_scan2 = cnt(sV)
        if "snapc0" in B.dbg:
            em("vector", lambda e: e.tensor_copy(
                msc[0:16, 832:1088], g16b[:, 0:256]))
        wait("scalar", sV, v_scan2)
        em("scalar", lambda e: e.activation(ggam[:], g16b[:], AF.Exp))
        v_gam = em("scalar", lambda e: e.activation(
            gigam[:], g16b[:], AF.Exp, scale=-1.0), inc=sS)
        wait("tensor", sV, v_cond)
        for tt in range(4):
            if tt > 0:
                wait("tensor", sV, cnt(sV))
            em("tensor", lambda e, tt=tt: e.matmul(
                PS[7][:, 0:512], dwsl[:, 3072:3200],
                condb[:, tt * 512:(tt + 1) * 512], start=True, stop=True),
                inc=sT)
            wait("vector", sT, cnt(sT))
            wait("vector", sS, v_sbr)
            em("vector", lambda e, tt=tt: e.tensor_scalar(
                g16a[:, tt * 512:(tt + 1) * 512], PS[7][0:16, 0:512],
                sb_bramp[:, 0:1], None, OP.mult), inc=sV)
        v_bd = cnt(sV)
        if "snapc0" in B.dbg:
            em("vector", lambda e: e.tensor_copy(
                msc[0:16, 1088:1280], g16a[:, 0:192]))
        wait("gpsimd", sS, v_gam)
        v_b0 = dma("gpsimd", g16b[:], gdb[16:32, :], sW)
        wait("vector", sW, v_b0)
        wait("vector", sV, v_bd)
        v_badd = em("vector", lambda e: (
            e.drain(),
            e.tensor_tensor(g16a[:], g16a[:], g16b[:], OP.add))[-1], inc=sV)
        wait("scalar", sV, v_badd)
        v_beta = em("scalar", lambda e: e.activation(
            gdb[0:16, :], g16a[:], AF.Sigmoid), inc=sS)

        B.dbg_store("snapc0", [128, 960], lambda o: (
            wait("gpsimd", sS, v_beta),
            dma("gpsimd", o, msc[:, 320:1280], sW)))
        B.dbg_store("gdb", [32, T], lambda o: (
            wait("gpsimd", sV, v_gdb),
            dma("gpsimd", o, gdb[:], sW)))
        B.dbg_store("glng", [16, T], lambda o: (
            wait("gpsimd", sS, v_lng),
            dma("gpsimd", o, g16a[:], sW)))
        B.dbg_store("gcum", [16, T], lambda o: (
            wait("gpsimd", sV, v_scan2),
            dma("gpsimd", o, g16b[:], sW)))
        B.dbg_store("gbeta", [16, T], lambda o: (
            wait("gpsimd", sS, v_beta),
            dma("gpsimd", o, gdb[0:16, :], sW)))
        B.dbg_store("ggam", [16, T], lambda o: (
            wait("gpsimd", sS, v_gam),
            dma("gpsimd", o, ggam[:], sW)))

    if stop not in ("condonly", "c0"):
        dbg_hp = 0
        lastmm = {}
        last_l2 = None
        prev_tt_ev = None
        halted = False
        for hp in range(8):
            if stop is not None and stop.startswith("hq") \
                    and hp >= int(stop[2:]):
                break
            FAM = [(hp, qpo, 0), (8 + hp, kpo, 1), (16 + hp, vpo, 2)]
            for fi, (oc, post, fam) in enumerate(FAM):
                if stop is not None and stop.startswith("hf") \
                        and hp == 1 and fi >= int(stop[2:]):
                    halted = True
                    break
                half = fi % 2
                wbase = half * 1024
                gkey = hp * 3 + fi
                if gkey == 0:
                    wait("sync", sT, v_c0mm)
                elif gkey == 1:
                    wait("sync", sT, cnt(sT))
                elif fi == 0:
                    wait("sync", sT, lastmm[gkey - 1])
                elif fi == 1:
                    wait("sync", sT, lastmm[gkey - 3])
                else:
                    wait("sync", sT, lastmm[gkey - 2])
                for dc in range(NDC):
                    vld = dma("sync", wstr[:, wbase + dc * 128:wbase + (dc + 1) * 128],
                              ctrlw_t[oc, dc], sCT[half])
                wait("tensor", sCT[half], vld)
                evs = {}
                for tt in range(4):
                    pbank = PS[tt % 2]
                    if tt >= 2:
                        wait("tensor", sV, evs[tt - 2])
                    elif gkey > 0:
                        wait("tensor", sV, prev_fam_evs[tt])
                    for dc in range(NDC):
                        em("tensor", lambda e, pbank=pbank, wbase=wbase,
                           dc=dc, tt=tt: e.matmul(
                               pbank[:, 0:512],
                               wstr[:, wbase + dc * 128:wbase + (dc + 1) * 128],
                               h8[dc][:, 3 + tt * 512:3 + tt * 512 + 512],
                               start=(dc == 0), stop=(dc == NDC - 1)),
                           inc=sT if dc == NDC - 1 else None)
                        if tt == 0:
                            em("tensor", lambda e, wbase=wbase, dc=dc: e.matmul(
                                PS[2][:, 0:3],
                                wstr[:, wbase + dc * 128:wbase + (dc + 1) * 128],
                                h8[dc][:, 0:3],
                                start=(dc == 0), stop=(dc == NDC - 1)),
                                inc=sT if dc == NDC - 1 else None)
                    wait("vector", sT, cnt(sT))
                    evs[tt] = em("vector", lambda e, pbank=pbank, tt=tt, oc=oc:
                                 e.tensor_scalar(
                                     pre[:, 3 + tt * 512:3 + tt * 512 + 512],
                                     pbank[:, 0:512],
                                     sb_ctrlb[:, oc:oc + 1], None, OP.add),
                                 inc=sV)
                lastmm[hp * 3 + fi] = cnt(sT)
                v_hev = em("vector", lambda e, oc=oc: e.tensor_scalar(
                    pre[:, 0:3], PS[2][:, 0:3],
                    sb_ctrlb[:, oc:oc + 1], None, OP.add), inc=sV)
                prev_fam_evs = {0: evs[2], 1: evs[3], 2: v_hev, 3: v_hev}
                for tt in range(4):
                    wait("tensor", sV, cnt(sV))
                    em("tensor", lambda e, fam=fam, tt=tt, hp=hp: e.matmul(
                        PS[3][:, 0:512], dwsl[:, (fam * 8 + hp) * 128:
                                              (fam * 8 + hp + 1) * 128],
                        condb[:, tt * 512:(tt + 1) * 512],
                        start=True, stop=True), inc=sT)
                    wait("vector", sT, cnt(sT))
                    em("vector", lambda e, fam=fam, tt=tt, hp=hp: (
                        e.drain(),
                        e.scalar_tensor_tensor(
                            pre[:, 3 + tt * 512:3 + tt * 512 + 512],
                            PS[3][:, 0:512],
                            sb_ramp[:, fam * 8 + hp:fam * 8 + hp + 1],
                            pre[:, 3 + tt * 512:3 + tt * 512 + 512],
                            OP.mult, OP.add))[-1], inc=sV)
                wait("tensor", sV, v_condh)
                wait("tensor", sV, cnt(sV))
                em("tensor", lambda e, fam=fam, hp=hp: e.matmul(
                    PS[3][:, 0:3], dwsl[:, (fam * 8 + hp) * 128:
                                         (fam * 8 + hp + 1) * 128],
                    CONDH, start=True, stop=True), inc=sT)
                wait("vector", sT, cnt(sT))
                v_preh = em("vector", lambda e, fam=fam, hp=hp: (
                    e.drain(),
                    e.scalar_tensor_tensor(
                        pre[:, 0:3], PS[3][:, 0:3],
                        sb_ramp[:, fam * 8 + hp:fam * 8 + hp + 1],
                        pre[:, 0:3], OP.mult, OP.add),
                    e.drain(),
                    e.tensor_scalar(pre[:, 0:3], pre[:, 0:3],
                                    sb_hmask[:, 0:1], None, OP.mult))[-1],
                    inc=sV)
                if hp == 0 and fi == 0:
                    wait("vector", sT, v_rms_mm)
                if hp == 0 and fi == 1:
                    wait("vector", sS, v_lnx)
                cw = fam * 32 + hp * 4
                v_conv = em("vector", lambda e, post=post, cw=cw: (
                    e.drain(),
                    e.tensor_scalar(post[:, 0:T], pre[:, 0:T],
                                    sb_convw[:, cw:cw + 1], None, OP.mult),
                    e.drain(),
                    e.scalar_tensor_tensor(post[:, 0:T], pre[:, 1:1 + T],
                                           sb_convw[:, cw + 1:cw + 2],
                                           post[:, 0:T], OP.mult, OP.add),
                    e.drain(),
                    e.scalar_tensor_tensor(post[:, 0:T], pre[:, 2:2 + T],
                                           sb_convw[:, cw + 2:cw + 3],
                                           post[:, 0:T], OP.mult, OP.add),
                    e.drain(),
                    e.scalar_tensor_tensor(post[:, 0:T], pre[:, 3:3 + T],
                                           sb_convw[:, cw + 3:cw + 4],
                                           post[:, 0:T], OP.mult, OP.add))[-1],
                    inc=sV)
                wait("scalar", sV, v_conv)
                v_silu = em("scalar", lambda e, post=post: e.activation(
                    post[:, 0:T], post[:, 0:T], AF.Silu), inc=sS)
                if fam < 2:
                    SQZ = msc[:, 320:832]
                    wait("scalar", sS, v_silu)
                    if last_l2 is not None:
                        wait("tensor", sS, last_l2)
                    for tt in range(4):
                        sl2 = slice(tt * 512, (tt + 1) * 512)
                        if prev_tt_ev is not None:
                            wait("scalar", sV, prev_tt_ev)
                        v_sq2 = em("scalar", lambda e, post=post, sl2=sl2:
                                   e.activation(SQZ, post[:, sl2], AF.Square),
                                   inc=sS)
                        wait("tensor", sS, v_sq2)
                        em("tensor", lambda e, tt=tt: e.matmul(
                            PS[4 + tt][:, 0:512], sb_bdones[:],
                            SQZ, start=True, stop=True), inc=sT)
                        wait("scalar", sT, cnt(sT))
                        em("scalar", lambda e, tt=tt: e.activation(
                            SQZ, PS[4 + tt][:, 0:512], AF.Ln,
                            bias=sb_consts[:, C_EPS6:C_EPS6 + 1]))
                        v_rst = em("scalar", lambda e: e.activation(
                            SQZ, SQZ, AF.Exp, scale=-0.5), inc=sS)
                        wait("vector", sS, v_rst)
                        prev_tt_ev = em("vector", lambda e, post=post,
                                        sl2=sl2: (
                            e.drain(),
                            e.tensor_tensor(post[:, sl2], post[:, sl2],
                                            SQZ, OP.mult))[-1], inc=sV)
                    last_l2 = v_rst
                else:
                    wait("vector", sS, v_silu)
            if halted:
                break
            v_qkv = cnt(sV)

            wait("tensor", sS, v_gam)
            wait("tensor", sS, v_beta)
            wait("tensor", sV, v_qkv)
            if last_l2 is not None:
                wait("tensor", sS, last_l2)
            for tt in range(4):
                sl = slice(tt * 512, (tt + 1) * 512)
                em("tensor", lambda e, sl=sl, hp=hp: e.matmul(
                    PS[5][:, 0:512], sb_expa2[:, hp * 128:(hp + 1) * 128],
                    ggam[:, sl], start=True, stop=True), inc=sT)
                em("tensor", lambda e, sl=sl, hp=hp: e.matmul(
                    PS[6][:, 0:512], sb_expa2[:, hp * 128:(hp + 1) * 128],
                    gigam[:, sl], start=True, stop=True), inc=sT)
                em("tensor", lambda e, sl=sl, hp=hp: e.matmul(
                    PS[7][:, 0:512], sb_expa2[:, hp * 128:(hp + 1) * 128],
                    gdb[0:16, sl], start=True, stop=True), inc=sT)
                wait("vector", sT, cnt(sT))
                snap_ps7 = (hp == 0 and tt == 0 and "psb" in B.dbg)
                vsc = em("vector", lambda e, sl=sl, tt=tt, snap=snap_ps7: (
                    e.drain(),
                    (e.tensor_copy(msc[:, 320:832], PS[7][:, 0:512])
                     if snap else e.nop()),
                    e.drain(),
                    e.tensor_copy(GCREP[:, tt * 8:tt * 8 + 8],
                                  PS[5][:, 63:512:64]),
                    e.tensor_tensor(vpo[:, sl], vpo[:, sl], PS[7][:, 0:512],
                                    OP.mult),
                    e.tensor_tensor(pre[:, sl], kpo[:, sl], PS[7][:, 0:512],
                                    OP.mult),
                    e.drain(),
                    e.tensor_tensor(pre[:, sl], pre[:, sl], PS[5][:, 0:512],
                                    OP.mult),
                    e.tensor_tensor(qpo[:, sl], qpo[:, sl], PS[5][:, 0:512],
                                    OP.mult),
                    e.tensor_tensor(kpo[:, sl], kpo[:, sl], PS[6][:, 0:512],
                                    OP.mult))[-1], inc=sV)
                if snap_ps7:
                    B.dbg_store("psb", [2, 128, 512], lambda o: (
                        wait("gpsimd", sV, vsc),
                        dma("gpsimd", o[0], msc[:, 320:832], sW),
                        dma("gpsimd", o[1], vpo[:, 0:512], sW),
                        wait("vector", sW, cnt(sW))))
                wait("tensor", sV, vsc)
            v_scaled = cnt(sV)

            if hp == dbg_hp:
                B.dbg_store("qg0", [128, T], lambda o: (
                    wait("gpsimd", sV, v_scaled),
                    dma("gpsimd", o, qpo[:, 0:T], sW)))
                B.dbg_store("kb0", [128, T], lambda o: (
                    wait("gpsimd", sV, v_scaled),
                    dma("gpsimd", o, kpo[:, 0:T], sW)))
                B.dbg_store("kh0", [128, T], lambda o: (
                    wait("gpsimd", sV, v_scaled),
                    dma("gpsimd", o, pre[:, 0:T], sW)))
                B.dbg_store("vb0", [128, T], lambda o: (
                    wait("gpsimd", sV, v_scaled),
                    dma("gpsimd", o, vpo[:, 0:T], sW)))
                B.dbg_store("gcrep0", [128, 32], lambda o: (
                    wait("gpsimd", sV, v_scaled),
                    dma("gpsimd", o, GCREP[:], sW)))
            if stop == "qkv":
                break

            emit_chunks(hp)
            if hp == dbg_hp:
                def _dump_sp(o):
                    wait("gpsimd", sSP, cnt(sSP))
                    wait("gpsimd", sV, cnt(sV))
                    for ci in range(NCH):
                        dma("gpsimd", o[ci], spill[0, ci], sW)
                B.dbg_store("spills", [NCH, 128, 320], _dump_sp)
                B.dbg_store("gacc", [128, 64], lambda o: (
                    wait("gpsimd", sV, cnt(sV)),
                    dma("gpsimd", o, GACC, sW)))
                B.dbg_store("hacc", [128, 64], lambda o: (
                    wait("gpsimd", sV, cnt(sV)),
                    dma("gpsimd", o, HACC, sW)))
            if stop is not None and stop.startswith("ck"):
                break
            if stop is not None and stop.startswith("hp") \
                    and hp + 1 >= int(stop[2:]):
                break
            if stop == "chunks" and hp == 0:
                break
            vcm = cnt(sV)
            wait("tensor", sV, vcm)
            em("tensor", lambda e: (e.transpose(
                PS[5][0:64, 0:64], GACC[0:64, :], sb_identA[0:64, :]),
                e.drain())[-1])
            tgt = em("tensor", lambda e: (e.transpose(
                PS[5][0:64, 64:128], GACC[64:128, :], sb_identB[64:128, :]),
                e.drain(), e.nop())[-1], inc=sT)
            wait("vector", sT, tgt)
            vgtb = em("vector", lambda e: (
                e.drain(),
                e.tensor_copy(tmp3[0:64, 0:64], PS[5][0:64, 64:128]))[-1],
                inc=sV)
            wait("tensor", sV, vgtb)
            tgt2 = em("tensor", lambda e: e.matmul(
                PS[5][64:128, 0:64], sb_identA[0:64, :], tmp3[0:64, 0:64],
                start=True, stop=True), inc=sT)
            wait("vector", sT, tgt2)
            vgt = em("vector", lambda e: (
                e.drain(),
                e.tensor_copy(msc[:, 5120:5184], PS[5][:, 0:64]))[-1], inc=sV)
            wait("gpsimd", sV, vgt)
            dma("gpsimd",
                cc2_in[0:1, hp * 8192:(hp + 1) * 8192].rearrange(
                    "a (p w) -> (a p) w", p=128),
                msc[:, 5120:5184], sW)
            dma("gpsimd",
                cc2_in[0:1, 65536 + hp * 8192:65536 + (hp + 1) * 8192]
                .rearrange("a (p w) -> (a p) w", p=128), HACC, sW)
            if hp == 0 and "hst0" in B.dbg:
                B.dbg_store("hst0", [128, 64], lambda o: (
                    dma("gpsimd", o, HACC, sW)))
                B.dbg_store("hcc0", [128, 64], lambda o: (
                    wait("gpsimd", sW, cnt(sW)),
                    dma("gpsimd", o,
                        cc2_in[0:1, 65536:65536 + 8192]
                        .rearrange("a (p w) -> (a p) w", p=128), sW)))

    if stop in (None, "x2", "attn"):
        v_allspill = cnt(sSP)
        v_cc2st = cnt(sW)
        wait("gpsimd", sW, v_cc2st)
        v_cc2 = em("gpsimd", lambda e: e.collective_compute(
            "AllGather", OP.bypass, ins=[cc2_in[:]], outs=[cc2_out[:]],
            replica_groups=[[0, 1, 2, 3, 4, 5, 6, 7]]), inc=sG)

        sGL = B.sem("sGL")
        wait("sync", sT, cnt(sT))
        wait("sync", sS, cnt(sS))
        wait("sync", sW, v_cc2st)
        v_gbl = dma("sync", msc[0:1, 5120:6144], gateb_row[:], sGL)
        wait("tensor", sGL, v_gbl)
        wait("tensor", sV, cnt(sV))
        wait("tensor", sS, cnt(sS))
        for half in range(2):
            em("tensor", lambda e, half=half: e.matmul(
                PS[6 + half][:, 0:512], sb_ones128[0:1, :],
                msc[0:1, 5120 + half * 512:5120 + (half + 1) * 512],
                start=True, stop=True), inc=sT)
        wait("vector", sT, cnt(sT))
        v_brep = em("vector", lambda e: (
            e.drain(),
            e.tensor_copy(CHA[:, 0:512], PS[6][:, 0:512]),
            e.tensor_copy(CHA[:, 512:1024], PS[7][:, 0:512]))[-1],
            inc=sV)
        stg = [condb[:, i * 256:(i + 1) * 256] for i in range(2)]
        sST = [B.sem(f"sST{i}") for i in range(2)]
        prev_mm = {}
        for g4 in range(4):
            for dc in range(NDC):
                if g4 > 0:
                    wait("sync", sT, prev_mm[g4 - 1])
                vw = dma("sync", wstr[:, dc * 256:(dc + 1) * 256],
                         wgate_tm[dc][:, g4 * 256:(g4 + 1) * 256], sCT[0])
            wait("tensor", sCT[0], vw)
            wait("tensor", sV, v_brep)
            gevs_prev = gevs if g4 > 0 else None
            gevs = {}
            for ti in range(NXT):
                pb = PS[ti % 2]
                if ti >= 2:
                    wait("tensor", sV, gevs[ti - 2])
                elif g4 > 0:
                    wait("tensor", sV, gevs_prev[NXT - 2 + ti])
                for dc in range(NDC):
                    em("tensor", lambda e, pb=pb, dc=dc, ti=ti: e.matmul(
                        pb[:, 0:256], h8[dc][:, 3 + ti * 128:3 + ti * 128 + 128],
                        wstr[:, dc * 256:(dc + 1) * 256],
                        start=(dc == 0), stop=(dc == NDC - 1)),
                       inc=sT if dc == NDC - 1 else None)
                wait("vector", sT, cnt(sT))
                si = ti % 2
                wait("vector", sST[si], cnt(sST[si]))
                em("vector", lambda e, pb=pb, g4=g4, si=si: (
                    e.drain(),
                    e.scalar_tensor_tensor(
                        stg[si], pb[:, 0:256], 1.0,
                        CHA[:, g4 * 256:(g4 + 1) * 256], OP.mult, OP.add))[-1])
                gevs[ti] = em("vector", lambda e: (e.drain(), e.nop())[-1], inc=sV)
                wait("scalar", sV, gevs[ti])
                vsl = em("scalar", lambda e, si=si: e.activation(
                    stg[si], stg[si], AF.Silu), inc=sS)
                wait("scalar", sS, vsl)
                dma("scalar", gate_sp[ti][:, g4 * 256:(g4 + 1) * 256],
                    stg[si], sST[si])
            prev_mm[g4] = cnt(sT)
        v_gate_done = cnt(sST[0]) + 0

        def _dump_gates(o):
            wait("gpsimd", sST[0], cnt(sST[0]))
            wait("gpsimd", sST[1], cnt(sST[1]))
            for ti in range(NXT):
                dma("gpsimd", o[ti], gate_sp[ti], sW)
        B.dbg_store("gate0", [NXT, 128, D], _dump_gates)

        sRL = [B.sem(f"sRL{i}") for i in range(2)]
        sGG = B.sem("sGG")
        sOS = B.sem("sOS")
        o_sp = B.dintern("o_sp", [8, 128, T])
        SIN = msc[:, 5184:5248]
        GTB = msc[:, 5248:5312]
        HB = msc[:, 5312:5376]
        T2 = msc[:, 5120:5184]
        SBD = qpo[:, 0:128]
        OB = qpo[:, 256:384]
        OSQ = qpo[:, 384:512]
        RSD = qpo[:, 512:516]
        GST = [qpo[0:64, 640 + i * 128:640 + (i + 1) * 128] for i in range(2)]
        vrs_prev = None
        wait("gpsimd", sG, v_cc2)
        B.dbg_store("cc2o", [8, H * DH * DH * 2], lambda o: (
            dma("gpsimd", o, cc2_out[:], sW)))
        for hp in range(8):
            wait("vector", sOS, cnt(sOS))
            vz = em("vector", lambda e: (
                e.drain(), e.memset(SIN, 0.0), e.memset(SBD, 0.0))[-1],
                inc=sV)
            ts_prev = None
            for b_ in range(7):
                if ts_prev is not None:
                    wait("gpsimd", sT, ts_prev)
                v1 = dma("gpsimd", GTB,
                         cc2_out[b_:b_ + 1, hp * 8192:(hp + 1) * 8192]
                         .rearrange("a (p w) -> (a p) w", p=128), sW)
                v2 = dma("gpsimd", HB,
                         cc2_out[b_:b_ + 1, 65536 + hp * 8192:
                                 65536 + (hp + 1) * 8192]
                         .rearrange("a (p w) -> (a p) w", p=128), sW)
                wait("vector", sW, v2)
                vq = em("vector", lambda e, b_=b_: (
                    e.drain(),
                    e.tensor_scalar(GTB, GTB, sb_prevmask[:, b_:b_ + 1],
                                    None, OP.mult),
                    e.tensor_scalar(T2, sb_eye2[:],
                                    sb_prevmaskc[:, b_:b_ + 1], None, OP.mult),
                    e.tensor_scalar(HB, HB, sb_prevmask[:, b_:b_ + 1],
                                    None, OP.mult),
                    e.drain(),
                    e.tensor_tensor(GTB, GTB, T2, OP.add))[-1], inc=sV)
                wait("tensor", sV, vq)
                for lo, hi in ((0, 64), (64, 128)):
                    em("tensor", lambda e, lo=lo, hi=hi: e.matmul(
                        PS[5][lo:hi, 0:64], GTB[lo:hi, :], SIN[lo:hi, :],
                        start=True, stop=False))
                    em("tensor", lambda e, lo=lo, hi=hi: e.matmul(
                        PS[5][lo:hi, 0:64], sb_eye2[lo:hi, :], HB[lo:hi, :],
                        start=False, stop=True))
                ts_ = em("tensor", lambda e: (e.drain(), e.nop())[-1], inc=sT)
                ts_prev = ts_
                wait("vector", sT, ts_)
                em("vector", lambda e: (
                    e.drain(), e.tensor_copy(SIN, PS[5][:, 0:64]))[-1],
                    inc=sV)
            if hp == 0:
                B.dbg_store("sin0", [128, 64], lambda o: (
                    wait("gpsimd", sV, cnt(sV)),
                    dma("gpsimd", o, SIN, sW)))
            wait("gpsimd", sV, cnt(sV))
            for ci in range(NCH):
                s_ = ci % 2
                if ci >= 2:
                    wait("gpsimd", sT, rl_consumed[ci - 2])
                    wait("gpsimd", sV, rl_consumed_v[ci - 2])
                else:
                    wait("gpsimd", sV, cnt(sV))
                dma("gpsimd", CHA[:, s_ * 320:s_ * 320 + 320],
                    spill[hp, ci], sRL[s_])
                wait("scalar", sV, cnt(sV))
                vgs = dma("scalar",
                          GST[s_], gate_sp[ci // 2]
                          [64 * (ci % 2):64 * (ci % 2) + 64,
                           hp * 128:(hp + 1) * 128], sGG)
                if ci == 0:
                    rl_consumed = {}
                    rl_consumed_v = {}
                vsb = em("vector", lambda e: (
                    e.drain(),
                    e.tensor_copy(SBD[0:64, 0:64], SIN[0:64, :]),
                    e.tensor_copy(SBD[64:128, 64:128], SIN[64:128, :]))[-1],
                    inc=sV)
                if hp == 0 and ci == 1 and "chn" in B.dbg:
                    B.dbg_store("chn", [5, 128, 64], lambda o: (
                        wait("gpsimd", sV, vsb),
                        wait("gpsimd", sRL[1], cnt(sRL[1])),
                        dma("gpsimd", o[0], SIN, sW),
                        dma("gpsimd", o[1], CHA[:, 320:384], sW),
                        dma("gpsimd", o[2], CHA[:, 384:448], sW),
                        dma("gpsimd", o[3], CHA[:, 448:512], sW),
                        dma("gpsimd", o[4, 0:64], CHA[0:64, 512:576], sW),
                        wait("vector", sW, cnt(sW))))
                wait("tensor", sV, vsb)
                wait("tensor", sRL[s_], cnt(sRL[s_]))
                em("tensor", lambda e, s_=s_: (e.matmul(
                    PS[4][0:64, 0:128],
                    CHA[0:64, s_ * 320 + 128:s_ * 320 + 192],
                    SBD[0:64, :], start=True, stop=False), e.drain())[-1])
                em("tensor", lambda e, s_=s_: (e.matmul(
                    PS[4][0:64, 0:128],
                    CHA[64:128, s_ * 320 + 128:s_ * 320 + 192],
                    SBD[64:128, :], start=False, stop=True),
                    e.drain())[-1])
                for lo, hi in ((0, 64), (64, 128)):
                    em("tensor", lambda e, s_=s_, lo=lo, hi=hi: e.matmul(
                        PS[5][lo:hi, 64:128],
                        CHA[lo:hi, s_ * 320:s_ * 320 + 64], SIN[lo:hi, :],
                        start=True, stop=False))
                    em("tensor", lambda e, s_=s_, lo=lo, hi=hi: e.matmul(
                        PS[5][lo:hi, 64:128], sb_eye2[lo:hi, :],
                        CHA[lo:hi, s_ * 320 + 64:s_ * 320 + 128],
                        start=False, stop=True))
                tcc = em("tensor", lambda e: (e.drain(), e.nop())[-1], inc=sT)
                rl_consumed[ci] = cnt(sT)
                wait("vector", sT, tcc)
                wait("vector", sGG, vgs)
                snap_ob = (hp == 0 and ci == 1 and "chn2" in B.dbg)
                if vrs_prev is not None:
                    wait("vector", sS, vrs_prev)
                em("vector", lambda e, s_=s_, snap=snap_ob: (
                    e.drain(),
                    e.tensor_copy(SIN, PS[5][:, 64:128]),
                    e.scalar_tensor_tensor(
                        OB[0:64, :], PS[4][0:64, 0:128], 1.0,
                        CHA[0:64, s_ * 320 + 192:s_ * 320 + 320],
                        OP.mult, OP.add),
                    e.drain(),
                    (e.tensor_copy(qpo[:, 1024:1152], PS[4][:, 0:128])
                     if snap else e.nop()),
                    (e.tensor_copy(qpo[0:64, 1152:1280], OB[0:64, :])
                     if snap else e.nop()),
                    e.drain(),
                    e.tensor_tensor(OSQ[0:64, :], OB[0:64, :], OB[0:64, :],
                                    OP.mult),
                    e.drain(),
                    e.reduce_sum(RSD[0:64, 0:1], OSQ[0:64, 0:64], axis=AX.X),
                    e.reduce_sum(RSD[0:64, 1:2], OSQ[0:64, 64:128],
                                 axis=AX.X))[-1], inc=sV)
                rl_consumed_v[ci] = cnt(sV)
                if snap_ob:
                    B.dbg_store("chn2", [128, 256], lambda o: (
                        wait("gpsimd", sV, cnt(sV)),
                        dma("gpsimd", o, qpo[:, 1024:1280], sW),
                        wait("vector", sW, cnt(sW))))
                wait("scalar", sV, cnt(sV))
                vrs = em("scalar", lambda e: (
                    e.activation(RSD[0:64, 2:4], RSD[0:64, 0:2], AF.Ln,
                                 bias=sb_consts[0:64, C_EPS6:C_EPS6 + 1],
                                 scale=1.0 / DH),
                    e.activation(RSD[0:64, 2:4], RSD[0:64, 2:4], AF.Exp,
                                 scale=-0.5))[-1], inc=sS)
                vrs_prev = vrs
                wait("vector", sS, vrs)
                vep = em("vector", lambda e, s_=s_, snap=snap_ob: (
                    e.drain(),
                    (e.tensor_copy(qpo[0:64, 1280:1284], RSD[0:64, 0:4])
                     if snap else e.nop()),
                    e.tensor_scalar(OB[0:64, 0:64], OB[0:64, 0:64],
                                    RSD[0:64, 2:3], None, OP.mult),
                    e.tensor_scalar(OB[0:64, 64:128], OB[0:64, 64:128],
                                    RSD[0:64, 3:4], None, OP.mult),
                    e.drain(),
                    (e.tensor_copy(qpo[0:64, 1284:1412], GST[s_])
                     if snap else e.nop()),
                    e.tensor_tensor(OB[0:64, :], OB[0:64, :], GST[s_],
                                    OP.mult),
                    e.drain(),
                    (e.tensor_copy(qpo[0:64, 1412:1540], OB[0:64, :])
                     if snap else e.nop()))[-1], inc=sV)
                if snap_ob:
                    B.dbg_store("chn3", [128, 516], lambda o: (
                        wait("gpsimd", sV, vep),
                        dma("gpsimd", o[:, 0:516], qpo[:, 1024:1540], sW),
                        wait("vector", sW, cnt(sW))))
                wait("tensor", sV, vep)
                ttr = em("tensor", lambda e: (e.transpose(
                    PS[6][:, 0:64], OB[0:64, :], sb_identA[0:64, :]),
                    e.drain(), e.nop())[-1], inc=sT)
                wait("vector", sT, ttr)
                vvp = em("vector", lambda e, ci=ci: (
                    e.drain(),
                    e.tensor_copy(vpo[:, ci * 64:(ci + 1) * 64],
                                  PS[6][:, 0:64]))[-1], inc=sV)
                if hp == 0 and ci == 1 and "chn4" in B.dbg:
                    B.dbg_store("chn4", [128, 128], lambda o: (
                        wait("gpsimd", sV, vvp),
                        dma("gpsimd", o, vpo[:, 0:128], sW),
                        wait("vector", sW, cnt(sW))))
                if hp == 0 and ci == 1 and "chn5" in B.dbg:
                    B.dbg_store("chn5", [128, 128], lambda o: (
                        wait("gpsimd", sV, vvp),
                        dma("gpsimd", o, vpo[:, 0:128], sW)))
            wait("gpsimd", sV, cnt(sV))
            dma("gpsimd", o_sp[hp], vpo[:, 0:T], sOS)
            if hp == 0:
                B.dbg_store("ofm0", [128, T], lambda o: (
                    wait("gpsimd", sOS, cnt(sOS)),
                    dma("gpsimd", o, o_sp[0], sW)))

        sXL = [B.sem(f"sXL{i}") for i in range(2)]
        sOL = [B.sem(f"sOL{i}") for i in range(2)]
        sWP = B.sem("sWP")
        em("vector", lambda e: (
            e.drain(),
            e.tensor_tensor(msc[:, 120:128], sb_mod[:, 16:24], sb_opb[:],
                            OP.mult))[-1], inc=sV)
        WOPbufs = ([kpo[:, i * 128:(i + 1) * 128] for i in range(16)]
                   + [pre[:, i * 128:(i + 1) * 128] for i in range(16)]
                   + [condb[:, i * 128:(i + 1) * 128] for i in range(16)]
                   + [qpo[:, 1024 + i * 128:1024 + (i + 1) * 128]
                      for i in range(8)]
                   + [CHA[:, i * 128:(i + 1) * 128] for i in range(8)])
        wait("sync", sV, cnt(sV))
        for jc in range(NDC):
            for ic in range(NDC):
                dma("sync", WOPbufs[jc * NDC + ic], wop_t[jc, ic], sWP)
        v_wop = cnt(sWP)
        wait("vector", sWP, v_wop)

        def _scale_wop(e):
            e.drain()
            for i in range(64):
                e.tensor_scalar(WOPbufs[i], WOPbufs[i], sb_outnw[:, 0:1],
                                None, OP.mult)
            e.drain()
            return e.nop()
        v_wsc = em("vector", _scale_wop, inc=sV)

        x2ev = 0
        wait("gpsimd", sOS, cnt(sOS))
        slot_user = {}
        for t8 in range(8):
            for ic in range(NDC):
                s_ = ic % 2
                if s_ in slot_user:
                    wait("gpsimd", sT, slot_user[s_])
                vo = dma("gpsimd", wstr[:, s_ * 256:(s_ + 1) * 256],
                         o_sp[ic][:, t8 * 256:(t8 + 1) * 256], sOL[s_])
                wait("tensor", sOL[s_], vo)
                if ic == 0:
                    wait("tensor", sV, v_wsc)
                    wait("tensor", sV, x2ev)
                for jc in range(NDC):
                    em("tensor", lambda e, jc=jc, ic=ic, s_=s_: e.matmul(
                        PS[jc][:, 0:256], WOPbufs[jc * NDC + ic],
                        wstr[:, s_ * 256:(s_ + 1) * 256],
                        start=(ic == 0), stop=(ic == NDC - 1)),
                       inc=sT if jc == NDC - 1 else None)
                slot_user[s_] = cnt(sT)
            v_opmm = cnt(sT)
            wait("sync", sV, x2ev)
            for half in range(2):
                ti = t8 * 2 + half
                dst = qpo[:, 0:1024] if half == 0 else CHA[:, 1024:2048]
                dma("sync", dst, xg[ti * 128:(ti + 1) * 128, :], sXL[half])
            wait("vector", sT, v_opmm)
            for jc in range(NDC):
                em("vector", lambda e, jc=jc: (
                    e.drain(),
                    e.tensor_scalar(wstr[:, 1024:1280], PS[jc][:, 0:256],
                                    sb_mod[:, 16 + jc:17 + jc],
                                    msc[:, 120 + jc:121 + jc],
                                    OP.mult, OP.add))[-1], inc=sV)
                wait("tensor", sV, cnt(sV))
                for half in range(2):
                    em("tensor", lambda e, jc=jc, half=half: e.transpose(
                        PS[jc][:, half * 128:half * 128 + 128],
                        wstr[:, 1024 + half * 128:1024 + half * 128 + 128],
                        sb_ident[:]), inc=sT if half == 1 else None)
                wait("vector", sT, cnt(sT))
                wait("vector", sXL[0], cnt(sXL[0]))
                wait("vector", sXL[1], cnt(sXL[1]))
                for half in range(2):
                    ti = t8 * 2 + half
                    xsrc = qpo[:, 0:1024] if half == 0 else CHA[:, 1024:2048]
                    em("vector", lambda e, jc=jc, half=half, ti=ti,
                       xsrc=xsrc: (
                        e.drain(),
                        e.scalar_tensor_tensor(
                            h8[ti // 2][:, (ti % 2) * 1024 + jc * 128:
                                        (ti % 2) * 1024 + jc * 128 + 128],
                            PS[jc][:, half * 128:half * 128 + 128], 1.0,
                            xsrc[:, jc * 128:(jc + 1) * 128],
                            OP.mult, OP.add))[-1],
                       inc=sV if (jc == NDC - 1 and half == 1) else None)
            x2ev = cnt(sV)
        v_x2 = cnt(sV)
        B.dbg_store("x2t0", [128, D], lambda o: (
            wait("gpsimd", sV, v_x2),
            dma("gpsimd", o, h8[0][:, 0:1024], sW)))

    if stop is None:
        sEW = B.sem("sEW")
        sEH = B.sem("sEH")
        sER = B.sem("sER")
        sEQ = B.sem("sEQ")
        sEO = B.sem("sEO")
        sEC = B.sem("sEC")
        POOLS = [qpo[:, 0:2048], kpo[:, 0:2048], vpo[:, 0:2048],
                 pre[:, 0:2048], condb[:, 0:2048], dwsl[:, 0:2048],
                 msc[:, 1280:3328], msc[:, 3328:5376]]
        SCRA = msc[:, 5376:5888]
        STG2 = wstr[:, 0:1024]
        SQB = wstr[:, 1024:2048]
        COLS = msc
        R127 = COLS[:, 0:1]
        GMC = COLS[:, 1:2]
        SXC = COLS[:, 2:3]
        RSXC = COLS[:, 3:4]
        HMC = COLS[:, 4:5]
        SHC = COLS[:, 5:6]
        RSHC = COLS[:, 6:7]
        TMC = COLS[:, 7:8]
        AMX = COLS[:, 8:9]
        WSC = COLS[:, 9:10]
        WRS = COLS[:, 10:11]

        def _round_inplace(buf, scol, w=512):
            em("vector", lambda e: (
                e.drain(),
                e.tensor_copy(i32b[:, 0:w], buf),
                e.drain(),
                e.tensor_copy(scrq[:, 0:w], i32b[:, 0:w]),
                e.drain(),
                e.tensor_scalar(SCRA[:, 0:w], buf, 0.5, None, OP.subtract),
                e.drain(),
                e.tensor_tensor(SCRA[:, 0:w], SCRA[:, 0:w], scrq[:, 0:w],
                                OP.is_ge),
                e.drain(),
                e.tensor_tensor(scrq[:, 0:w], scrq[:, 0:w], SCRA[:, 0:w],
                                OP.add),
                e.drain(),
                e.tensor_scalar(SCRA[:, 0:w], buf, 0.5, None, OP.add),
                e.drain(),
                e.tensor_copy(buf, i32b[:, 0:w]),
                e.drain(),
                e.tensor_tensor(SCRA[:, 0:w], SCRA[:, 0:w], buf, OP.is_le),
                e.drain(),
                e.tensor_tensor(scrq[:, 0:w], scrq[:, 0:w], SCRA[:, 0:w],
                                OP.subtract),
                e.drain(),
                e.tensor_scalar(scrq[:, 0:w], scrq[:, 0:w], -128.0, 127.0,
                                OP.max, OP.min),
                e.drain(),
                e.tensor_scalar(buf, scrq[:, 0:w], scol, None,
                                OP.mult))[-1])

        wait("vector", sT, cnt(sT))
        em("vector", lambda e: (
            e.drain(), e.memset(R127, 127.0),
            e.drain(), e.memset(GMC, 0.0), e.memset(HMC, 0.0))[-1])
        for dc in range(NDC):
            em("vector", lambda e, dc=dc: e.tensor_scalar(
                msc[:, 112 + dc:113 + dc], sb_mod[:, 32 + dc:33 + dc], 1.0,
                None, OP.add))
        v_e0 = em("vector", lambda e: (e.drain(), e.nop())[-1], inc=sV)
        v_r127 = em("vector", lambda e: (
            e.drain(), e.reciprocal(R127, R127))[-1], inc=sV)

        wait("tensor", sV, v_x2)
        tr_ev = {}
        for ti in range(NXT):
            x2t = h8[ti // 2][:, (ti % 2) * 1024:(ti % 2) * 1024 + 1024]
            mcol = msc[:, 152 + ti:153 + ti]
            sqcol = msc[:, 168 + ti:169 + ti]
            vcol = msc[:, 184 + ti:185 + ti]
            rcol = msc[:, 200 + ti:201 + ti]
            em("vector", lambda e, x2t=x2t, mcol=mcol: (
                e.drain(), e.reduce_sum(mcol, x2t, axis=AX.X))[-1])
            vr = em("vector", lambda e: (e.drain(), e.nop())[-1], inc=sV)
            wait("scalar", sV, vr)
            vsq = em("scalar", lambda e, x2t=x2t, sqcol=sqcol: e.activation(
                SQB, x2t, AF.Square, accum_out=sqcol), inc=sS)
            wait("vector", sS, vsq)
            vfin = em("vector", lambda e, mcol=mcol, sqcol=sqcol, vcol=vcol: (
                e.drain(),
                e.tensor_scalar(mcol, mcol, 1.0 / D, None, OP.mult),
                e.drain(),
                e.tensor_scalar(vcol, sqcol, 1.0 / D, None, OP.mult),
                e.drain(),
                e.tensor_tensor(TMC, mcol, mcol, OP.mult),
                e.drain(),
                e.tensor_tensor(vcol, vcol, TMC, OP.subtract))[-1], inc=sV)
            wait("scalar", sV, vfin)
            vrs = em("scalar", lambda e, vcol=vcol, rcol=rcol: (
                e.activation(rcol, vcol, AF.Ln,
                             bias=sb_consts[:, C_EPS5:C_EPS5 + 1]),
                e.activation(rcol, rcol, AF.Exp, scale=-0.5))[-1], inc=sS)
            wait("vector", sS, vrs)
            if ti >= 1:
                wait("vector", sT, tr_ev[ti - 1])
            vap = em("vector", lambda e, x2t=x2t, mcol=mcol, rcol=rcol: (
                e.drain(),
                e.tensor_scalar(STG2, x2t, mcol, rcol,
                                OP.subtract, OP.mult))[-1], inc=sV)
            b0 = (ti % 2) * 2
            wait("tensor", sV, vap)
            if ti >= 2:
                wait("tensor", sV, tr_ev.get(("evac", ti - 2), 0))
            for dc in range(NDC):
                em("tensor", lambda e, dc=dc, b0=b0: e.transpose(
                    PS[b0 + dc // 4][:, (dc % 4) * 128:(dc % 4) * 128 + 128],
                    STG2[:, dc * 128:dc * 128 + 128], sb_ident[:]),
                    inc=sT if dc == NDC - 1 else None)
            tr_ev[ti] = cnt(sT)
            wait("vector", sT, tr_ev[ti])
            for dc in range(NDC):
                em("vector", lambda e, dc=dc, ti=ti, b0=b0: e.tensor_copy(
                    POOLS[dc][:, ti * 128:(ti + 1) * 128],
                    PS[b0 + dc // 4][:, (dc % 4) * 128:(dc % 4) * 128 + 128]),
                   inc=sV if dc == NDC - 1 else None)
            tr_ev[("evac", ti)] = cnt(sV)
        wait("vector", sT, cnt(sT))
        for dc in range(NDC):
            em("vector", lambda e, dc=dc: (
                e.drain(),
                e.tensor_scalar(POOLS[dc], POOLS[dc],
                                msc[:, 112 + dc:113 + dc],
                                sb_mod[:, 24 + dc:25 + dc],
                                OP.mult, OP.add))[-1])
        for dc in range(NDC):
            em("vector", lambda e, dc=dc: (
                e.drain(),
                e.tensor_reduce(TMC, POOLS[dc], AX.X, OP.max,
                                apply_absolute_value=True),
                e.drain(),
                e.tensor_tensor(GMC, GMC, TMC, OP.max))[-1])
        v_h2 = em("vector", lambda e: (e.drain(), e.nop())[-1], inc=sV)
        if "h2p0" in B.dbg:
            B.dbg_store("h2p0", [128, T], lambda o: (
                wait("gpsimd", sV, v_h2),
                dma("gpsimd", o, POOLS[0], sW),
                wait("vector", sW, cnt(sW))))

        wait("tensor", sV, v_h2)
        em("tensor", lambda e: e.transpose(
            PS[4][0:1, 0:128], GMC, sb_ident[:]), inc=sT)
        wait("vector", sT, cnt(sT))
        vgr = em("vector", lambda e: (
            e.drain(),
            e.tensor_copy(msc[0:1, 216:344], PS[4][0:1, 0:128]))[-1], inc=sV)
        wait("gpsimd", sV, vgr)
        vcc3s = dma("gpsimd", cc3_in[:], msc[0:1, 216:344], sW)
        wait("gpsimd", sW, vcc3s)
        vcc3 = em("gpsimd", lambda e: e.collective_compute(
            "AllGather", OP.bypass, ins=[cc3_in[:]], outs=[cc3_out[:]],
            replica_groups=[[0, 1, 2, 3, 4, 5, 6, 7]]), inc=sG)
        wait("gpsimd", sG, vcc3)
        vg1k = dma("gpsimd", wstr[0:1, 0:1024],
                   cc3_out.rearrange("a (c b) -> c (a b)", c=1), sEC)
        wait("vector", sEC, vg1k)
        wait("vector", sV, v_r127)
        vsx1 = em("vector", lambda e: (
            e.drain(),
            e.reduce_max(msc[0:1, 216:217], wstr[0:1, 0:1024],
                         axis=AX.X))[-1], inc=sV)
        wait("scalar", sV, vsx1)
        vsg = em("scalar", lambda e: e.activation(
            msc[0:1, 217:218], sb_clip2[0:1, 0:1], AF.Sigmoid), inc=sS)
        wait("vector", sS, vsg)
        vsx2 = em("vector", lambda e: (
            e.drain(),
            e.tensor_tensor(msc[0:1, 218:219], msc[0:1, 216:217],
                            msc[0:1, 217:218], OP.mult),
            e.drain(),
            e.tensor_scalar(msc[0:1, 218:219], msc[0:1, 218:219],
                            R127[0:1, 0:1], None, OP.mult))[-1], inc=sV)
        wait("tensor", sV, vsx2)
        em("tensor", lambda e: e.matmul(
            PS[4][:, 0:1], sb_ones128[0:1, :], msc[0:1, 218:219],
            start=True, stop=True), inc=sT)
        wait("vector", sT, cnt(sT))
        vsxc = em("vector", lambda e: (
            e.drain(), e.tensor_copy(SXC, PS[4][:, 0:1]))[-1], inc=sV)
        v_rsx = em("vector", lambda e: (
            e.drain(), e.reciprocal(RSXC, SXC))[-1], inc=sV)

        for dc in range(NDC):
            for st_ in range(4):
                strip = POOLS[dc][:, st_ * 512:(st_ + 1) * 512]
                em("vector", lambda e, strip=strip: (
                    e.drain(),
                    e.tensor_scalar(strip, strip, RSXC, None, OP.mult))[-1])
                _round_inplace(strip, SXC)
        v_e2 = em("vector", lambda e: (e.drain(), e.nop())[-1], inc=sV)
        if "xqp0" in B.dbg:
            B.dbg_store("xqp0", [128, T], lambda o: (
                wait("gpsimd", sV, v_e2),
                dma("gpsimd", o, POOLS[0], sW),
                wait("vector", sW, cnt(sW))))

        GT_E = msc[:, 256:1280]
        UT_E = wstr[:, 0:1024]
        prev_mm_ffc = None
        prev_hf = {}
        for ffc in range(NFC):
            if prev_mm_ffc is not None:
                wait("sync", sT, prev_mm_ffc)
            else:
                wait("sync", sV, v_e2)
            vwg = dma("sync", wstr[:, 0:1024],
                      wg_nat[ffc * 128:(ffc + 1) * 128, :], sEW)
            vwu = dma("sync", wstr[:, 1024:2048],
                      wu_nat[ffc * 128:(ffc + 1) * 128, :], sEW)
            wait("vector", sEW, vwu)
            if prev_mm_ffc is not None:
                wait("vector", sT, prev_mm_ffc)
            wait("vector", sEH, cnt(sEH))
            for wi in range(2):
                whalf = wstr[:, wi * 1024:(wi + 1) * 1024]
                em("vector", lambda e, whalf=whalf: (
                    e.drain(),
                    e.tensor_reduce(AMX, whalf, AX.X, OP.max,
                                    apply_absolute_value=True),
                    e.drain(),
                    e.tensor_scalar(WSC, AMX, R127, None, OP.mult))[-1])
                em("vector", lambda e, whalf=whalf: (
                    e.drain(),
                    e.reciprocal(WRS, WSC),
                    e.drain(),
                    e.tensor_scalar(whalf, whalf, WRS, None, OP.mult))[-1])
                for st_ in range(2):
                    _round_inplace(whalf[:, st_ * 512:(st_ + 1) * 512], WSC)
            v_wq = em("vector", lambda e: (e.drain(), e.nop())[-1], inc=sV)
            wait("tensor", sV, v_wq)
            for dc in range(NDC):
                em("tensor", lambda e, dc=dc: e.transpose(
                    PS[dc // 4][:, (dc % 4) * 128:(dc % 4) * 128 + 128],
                    wstr[:, dc * 128:dc * 128 + 128], sb_ident[:]),
                    inc=sT if dc == NDC - 1 else None)
            vtg = cnt(sT)
            wait("vector", sT, vtg)
            for dc in range(NDC):
                em("vector", lambda e, dc=dc: e.tensor_copy(
                    GT_E[:, dc * 128:dc * 128 + 128],
                    PS[dc // 4][:, (dc % 4) * 128:(dc % 4) * 128 + 128]))
            vge = em("vector", lambda e: (e.drain(), e.nop())[-1], inc=sV)
            wait("tensor", sV, vge)
            for dc in range(NDC):
                em("tensor", lambda e, dc=dc: e.transpose(
                    PS[2 + dc // 4][:, (dc % 4) * 128:(dc % 4) * 128 + 128],
                    wstr[:, 1024 + dc * 128:1024 + dc * 128 + 128],
                    sb_ident[:]), inc=sT if dc == NDC - 1 else None)
            vtu = cnt(sT)
            wait("vector", sT, vtu)
            for dc in range(NDC):
                em("vector", lambda e, dc=dc: e.tensor_copy(
                    UT_E[:, dc * 128:dc * 128 + 128],
                    PS[2 + dc // 4][:, (dc % 4) * 128:(dc % 4) * 128 + 128]))
            vue = em("vector", lambda e: (e.drain(), e.nop())[-1], inc=sV)
            wait("tensor", sV, vue)
            if prev_hf:
                wait("tensor", sS, prev_hf["silu"])
                wait("tensor", sV, prev_hf["done"])
            for tt in range(4):
                for dc in range(NDC):
                    em("tensor", lambda e, tt=tt, dc=dc: e.matmul(
                        PS[tt][:, 0:512], GT_E[:, dc * 128:dc * 128 + 128],
                        POOLS[dc][:, tt * 512:(tt + 1) * 512],
                        start=(dc == 0), stop=(dc == NDC - 1)),
                       inc=sT if dc == NDC - 1 else None)
                for dc in range(NDC):
                    em("tensor", lambda e, tt=tt, dc=dc: e.matmul(
                        PS[4 + tt][:, 0:512], UT_E[:, dc * 128:dc * 128 + 128],
                        POOLS[dc][:, tt * 512:(tt + 1) * 512],
                        start=(dc == 0), stop=(dc == NDC - 1)),
                       inc=sT if dc == NDC - 1 else None)
            prev_mm_ffc = cnt(sT)
            silu_evs = None
            for tt in range(4):
                wait("scalar", sT, prev_mm_ffc)
                if tt > 0 or ffc > 0:
                    wait("scalar", sEH, cnt(sEH))
                vsl = em("scalar", lambda e, tt=tt: e.activation(
                    SCRA, PS[tt][:, 0:512], AF.Silu), inc=sS)
                silu_evs = vsl
                wait("vector", sS, vsl)
                vhf = em("vector", lambda e, tt=tt: (
                    e.drain(),
                    e.tensor_tensor(SCRA, SCRA, PS[4 + tt][:, 0:512],
                                    OP.mult),
                    e.drain(),
                    e.tensor_reduce(TMC, SCRA, AX.X, OP.max,
                                    apply_absolute_value=True),
                    e.drain(),
                    e.tensor_tensor(HMC, HMC, TMC, OP.max))[-1], inc=sV)
                wait("gpsimd", sV, vhf)
                dma("gpsimd", hffn_sp[ffc][:, tt * 512:(tt + 1) * 512],
                    SCRA, sEH)
            prev_hf = {"silu": silu_evs, "done": cnt(sV)}
        v_e3 = cnt(sV)
        if "hf0" in B.dbg:
            B.dbg_store("hf0", [128, T], lambda o: (
                wait("gpsimd", sEH, cnt(sEH)),
                dma("gpsimd", o, hffn_sp[0], sW)))

        wait("tensor", sV, v_e3)
        em("tensor", lambda e: e.transpose(
            PS[2][0:1, 0:128], HMC, sb_ident[:]), inc=sT)
        v_tr_h = cnt(sT)
        wait("vector", sT, v_tr_h)
        vhr = em("vector", lambda e: (
            e.drain(),
            e.tensor_copy(msc[0:1, 344:472], PS[2][0:1, 0:128]))[-1], inc=sV)
        wait("gpsimd", sV, vhr)
        wait("gpsimd", sEH, cnt(sEH))
        vcc4s = dma("gpsimd", cc4_in[:], msc[0:1, 344:472], sW)
        wait("gpsimd", sW, vcc4s)
        vcc4 = em("gpsimd", lambda e: e.collective_compute(
            "AllGather", OP.bypass, ins=[cc4_in[:]], outs=[cc4_out[:]],
            replica_groups=[[0, 1, 2, 3, 4, 5, 6, 7]]), inc=sG)
        wait("gpsimd", sG, vcc4)
        vh1k = dma("gpsimd", msc[0:1, 4224:5248],
                   cc4_out.rearrange("a (c b) -> c (a b)", c=1), sEC)

        WDSL = msc[:, 1280:4096]
        EBUF = msc[:, 5888:6016]
        prev_tr = None
        for dc in range(NDC):
            if dc == 0:
                wait("sync", sV, v_e3)
            else:
                wait("sync", sT, prev_tr)
            vwd = dma("sync", WDSL, wd_nat[dc * 128:(dc + 1) * 128, :], sEW)
            wait("vector", sEW, vwd)
            em("vector", lambda e: (
                e.drain(),
                e.tensor_reduce(AMX, WDSL, AX.X, OP.max,
                                apply_absolute_value=True),
                e.drain(),
                e.tensor_scalar(WSC, AMX, R127, None, OP.mult))[-1])
            em("vector", lambda e: (
                e.drain(),
                e.reciprocal(WRS, WSC),
                e.drain(),
                e.tensor_scalar(WDSL, WDSL, WRS, None, OP.mult))[-1])
            for st_ in range(6):
                lo = st_ * 512
                hi = min(lo + 512, FF)
                _round_inplace(WDSL[:, lo:hi], WSC, w=hi - lo)
            v_wdq = em("vector", lambda e: (e.drain(), e.nop())[-1], inc=sV)
            wait("tensor", sV, v_wdq)
            vev_hist = {}
            for ffc in range(NFC):
                if ffc >= 2:
                    wait("tensor", sV, vev_hist[ffc - 2])
                em("tensor", lambda e, ffc=ffc: e.transpose(
                    PS[ffc % 2][:, 0:128],
                    WDSL[:, ffc * 128:ffc * 128 + 128], sb_ident[:]),
                    inc=sT)
                tev = cnt(sT)
                wait("vector", sT, tev)
                wait("vector", sEQ, cnt(sEQ))
                vev = em("vector", lambda e, ffc=ffc: (
                    e.drain(),
                    e.tensor_copy(EBUF, PS[ffc % 2][:, 0:128]))[-1], inc=sV)
                vev_hist[ffc] = vev
                wait("gpsimd", sV, vev)
                dma("gpsimd", wdq_sp[ffc, dc], EBUF, sEQ)
            prev_tr = cnt(sT)

        wait("vector", sEC, vh1k)
        vsh1 = em("vector", lambda e: (
            e.drain(),
            e.reduce_max(msc[0:1, 4224:4225], msc[0:1, 4224:5248],
                         axis=AX.X))[-1], inc=sV)
        wait("scalar", sV, vsh1)
        vshg = em("scalar", lambda e: e.activation(
            msc[0:1, 4225:4226], sb_clip2[0:1, 1:2], AF.Sigmoid), inc=sS)
        wait("vector", sS, vshg)
        vsh2 = em("vector", lambda e: (
            e.drain(),
            e.tensor_tensor(msc[0:1, 4226:4227], msc[0:1, 4224:4225],
                            msc[0:1, 4225:4226], OP.mult),
            e.drain(),
            e.tensor_scalar(msc[0:1, 4226:4227], msc[0:1, 4226:4227],
                            R127[0:1, 0:1], None, OP.mult))[-1], inc=sV)
        wait("tensor", sV, vsh2)
        wait("tensor", sEQ, cnt(sEQ))
        em("tensor", lambda e: e.matmul(
            PS[4][:, 0:1], sb_ones128[0:1, :], msc[0:1, 4226:4227],
            start=True, stop=True), inc=sT)
        wait("vector", sT, cnt(sT))
        vshc = em("vector", lambda e: (
            e.drain(), e.tensor_copy(SHC, PS[4][:, 0:1]))[-1], inc=sV)
        v_rsh = em("vector", lambda e: (
            e.drain(), e.reciprocal(RSHC, SHC))[-1], inc=sV)

        gbrow_sp = B.dintern("gbrow_sp", [8, 128])
        em("tensor", lambda e: e.transpose(
            PS[2][0:8, 0:128], sb_mod[:, 40:48], sb_ident[:]), inc=sT)
        wait("vector", sT, cnt(sT))
        vgbt = em("vector", lambda e: (
            e.drain(),
            e.tensor_copy(msc[0:8, 4352:4480], PS[2][0:8, 0:128]))[-1],
            inc=sV)
        wait("gpsimd", sV, vgbt)
        vgb1 = dma("gpsimd", gbrow_sp[:], msc[0:8, 4352:4480], sEC)
        wait("gpsimd", sEC, vgb1)
        vgb2 = dma("gpsimd", wstr[0:1, 0:1024],
                   gbrow_sp.rearrange("a (c b) -> c (a b)", c=1), sEC)
        wait("tensor", sEC, vgb2)
        for half in range(2):
            em("tensor", lambda e, half=half: e.matmul(
                PS[2 + half][:, 0:512], sb_ones128[0:1, :],
                wstr[0:1, half * 512:(half + 1) * 512],
                start=True, stop=True), inc=sT)
        wait("vector", sT, cnt(sT))
        vgbr = em("vector", lambda e: (
            e.drain(),
            e.tensor_copy(dwsl[:, 0:512], PS[2][:, 0:512]),
            e.tensor_copy(dwsl[:, 512:1024], PS[3][:, 0:512]))[-1], inc=sV)
        v_gbrep = cnt(sV)
        GBREP = dwsl[:, 0:1024]

        WDT = msc[:, 256:1280]
        OUTB = wstr[:, 1024:2048]
        v_out_ev = 0
        prev_dmm = None
        wait("gpsimd", sEH, cnt(sEH))
        wait("gpsimd", sEQ, cnt(sEQ))
        for tg in range(4):
            for ffc in range(NFC):
                if prev_dmm is not None:
                    wait("gpsimd", sT, prev_dmm)
                else:
                    wait("gpsimd", sV, v_gbrep)
                for dcl in range(NDC):
                    dma("gpsimd", WDT[:, dcl * 128:dcl * 128 + 128],
                        wdq_sp[ffc, dcl], sER)
                vwdt = cnt(sER)
                vhfl = dma("gpsimd", SCRA,
                           hffn_sp[ffc][:, tg * 512:(tg + 1) * 512], sER)
                wait("vector", sER, vhfl)
                if ffc == 0:
                    wait("vector", sEO, cnt(sEO))
                em("vector", lambda e: (
                    e.drain(),
                    e.tensor_scalar(SCRA, SCRA, RSHC, None, OP.mult))[-1])
                em("vector", lambda e: (
                    e.drain(),
                    e.tensor_copy(i32b[:, 0:512], SCRA),
                    e.drain(),
                    e.tensor_copy(scrq[:, 0:512], i32b[:, 0:512]),
                    e.drain(),
                    e.tensor_scalar(OUTB[:, 0:512], SCRA, 0.5, None,
                                    OP.subtract),
                    e.drain(),
                    e.tensor_tensor(OUTB[:, 0:512], OUTB[:, 0:512],
                                    scrq[:, 0:512], OP.is_ge),
                    e.drain(),
                    e.tensor_tensor(scrq[:, 0:512], scrq[:, 0:512],
                                    OUTB[:, 0:512], OP.add),
                    e.drain(),
                    e.tensor_scalar(OUTB[:, 0:512], SCRA, 0.5, None, OP.add),
                    e.drain(),
                    e.tensor_copy(SCRA, i32b[:, 0:512]),
                    e.drain(),
                    e.tensor_tensor(OUTB[:, 0:512], OUTB[:, 0:512], SCRA,
                                    OP.is_le),
                    e.drain(),
                    e.tensor_tensor(scrq[:, 0:512], scrq[:, 0:512],
                                    OUTB[:, 0:512], OP.subtract),
                    e.drain(),
                    e.tensor_scalar(scrq[:, 0:512], scrq[:, 0:512],
                                    -128.0, 127.0, OP.max, OP.min),
                    e.drain(),
                    e.tensor_scalar(SCRA, scrq[:, 0:512], SHC, None,
                                    OP.mult))[-1])
                vhq = em("vector", lambda e: (e.drain(), e.nop())[-1], inc=sV)
                wait("tensor", sV, vhq)
                wait("tensor", sER, vwdt)
                if ffc == 0:
                    wait("tensor", sV, v_gbrep)
                    wait("tensor", sV, v_out_ev)
                for t4 in range(4):
                    for hf_ in range(2):
                        em("tensor", lambda e, t4=t4, hf_=hf_, ffc=ffc:
                           e.matmul(
                               PS[t4 * 2 + hf_][:, 0:512],
                               SCRA[:, t4 * 128:t4 * 128 + 128],
                               WDT[:, hf_ * 512:(hf_ + 1) * 512],
                               start=(ffc == 0), stop=(ffc == NFC - 1)),
                           inc=sT if (t4 == 3 and hf_ == 1) else None)
                prev_dmm = cnt(sT)
            v_dmm = cnt(sT)
            wait("vector", sT, v_dmm)
            for t4 in range(4):
                ti = tg * 4 + t4
                x2t = h8[ti // 2][:, (ti % 2) * 1024:(ti % 2) * 1024 + 1024]
                wait("vector", sEO, cnt(sEO))
                em("vector", lambda e, t4=t4, x2t=x2t: (
                    e.drain(),
                    e.tensor_tensor(OUTB[:, 0:512], PS[t4 * 2][:, 0:512],
                                    GBREP[:, 0:512], OP.mult),
                    e.drain(),
                    e.tensor_tensor(OUTB[:, 0:512], OUTB[:, 0:512],
                                    x2t[:, 0:512], OP.add),
                    e.drain(),
                    e.tensor_tensor(OUTB[:, 512:1024],
                                    PS[t4 * 2 + 1][:, 0:512],
                                    GBREP[:, 512:1024], OP.mult),
                    e.drain(),
                    e.tensor_tensor(OUTB[:, 512:1024], OUTB[:, 512:1024],
                                    x2t[:, 512:1024], OP.add))[-1], inc=sV)
                wait("gpsimd", sV, cnt(sV))
                ti_ = tg * 4 + t4
                dma("gpsimd", out[ti_ * 128:(ti_ + 1) * 128, :], OUTB, sEO)
            v_out_ev = cnt(sV)
        wait("vector", sEO, cnt(sEO))
    else:
        wait("gpsimd", sV, v_h_done)
        wait("gpsimd", sV, v_cond)
        wait("gpsimd", sV, v_condh)
        for i in range(NXT):
            dma("gpsimd", out[i * 128:(i + 1) * 128, :],
                msc[:, M_XTM:M_XTM + 1024], sW)
        wait("gpsimd", sW, cnt(sW))

    return B.finish(), B.dbg_outs


def host_prep(inputs):
    f = np.ascontiguousarray
    x = np.asarray(inputs["L0_attn_hidden"], np.float32).reshape(16384, D)
    wmod = np.asarray(inputs["L0_wmod"], np.float32)
    wmod_t = f(wmod.reshape(NDC, 128, 48, 128).transpose(2, 0, 1, 3))
    cvec = np.asarray(inputs["L0_c"], np.float32).reshape(D)
    cvec_fm = f(cvec.reshape(NDC, 128).T)
    condw = np.asarray(inputs["L0_attn_cond_in_W"], np.float32)
    condw_t = f(condw.reshape(NDC, 128, 128))
    theta = np.asarray(inputs["L0_attn_rot_theta"], np.float32)
    theta_col = f(np.repeat(theta, 2)[:, None])
    cnw_col = f(np.asarray(inputs["L0_attn_cond_norm_w"], np.float32)[:, None])
    dW = np.asarray(inputs["L0_attn_delta_W"], np.float32)
    deltaw_sl = np.zeros((128, 3200), np.float32)
    deltaw_sl[:, :3088] = dW
    ramp = np.asarray(inputs["L0_attn_qkv_ramp_logit"], np.float32)
    ramp_fm = f(ramp.reshape(3, NDC, 128).transpose(2, 0, 1).reshape(128, 24))
    bramp_col = f(np.asarray(
        inputs["L0_attn_beta_ramp_logit"], np.float32)[:, None])
    cW = np.asarray(inputs["L0_attn_ctrl_W"], np.float32)
    cWp = np.zeros((D, CTRL_OC * 128), np.float32)
    cWp[:, :4128] = cW
    ctrlw_t = f(cWp.reshape(NDC, 128, CTRL_OC, 128).transpose(2, 0, 1, 3))
    cb = np.asarray(inputs["L0_attn_ctrl_b"], np.float32)
    cbp = np.zeros(CTRL_OC * 128, np.float32)
    cbp[:4128] = cb
    ctrlb_fm = f(cbp.reshape(CTRL_OC, 128).T)
    wgate_tm = f(cW[:, GCOL:GCOL + D].reshape(NDC, 128, D))
    gateb_row = f(cb[GCOL:GCOL + D][None, :])
    cwq = np.asarray(inputs["L0_attn_q_conv_w"], np.float32)
    cwk = np.asarray(inputs["L0_attn_k_conv_w"], np.float32)
    cwv = np.asarray(inputs["L0_attn_v_conv_w"], np.float32)
    convw = f(np.concatenate(
        [w.reshape(NDC, 128, 4).transpose(1, 0, 2).reshape(128, 32)
         for w in (cwq, cwk, cwv)], axis=1))
    onw = np.asarray(inputs["L0_attn_out_norm_w"], np.float32)
    outnw_col = f(np.tile(onw, 2)[:, None])
    opW = np.asarray(inputs["L0_attn_out_proj_W"], np.float32)
    wop_t = f(opW.T.reshape(NDC, 128, NDC, 128).transpose(2, 0, 1, 3))
    opb_fm = f(np.asarray(
        inputs["L0_attn_out_proj_b"], np.float32).reshape(NDC, 128).T)
    wg_nat = f(np.asarray(inputs["L0_ffn_w_gate"], np.float32))
    wu_nat = f(np.asarray(inputs["L0_ffn_w_up"], np.float32))
    wd_nat = f(np.asarray(inputs["L0_ffn_w_down"], np.float32))
    clip2 = f(np.array([[np.asarray(inputs["L0_ffn_clip_x"], np.float32)[0],
                         np.asarray(inputs["L0_ffn_clip_down"], np.float32)[0]]],
                       np.float32))
    ident = np.eye(128, dtype=np.float32)
    identA = np.zeros((128, 64), np.float32)
    identA[0:64] = np.eye(64)
    identB = np.zeros((128, 64), np.float32)
    identB[64:128] = np.eye(64)
    bdones = np.zeros((128, 128), np.float32)
    bdones[0:64, 0:64] = 1
    bdones[64:128, 64:128] = 1
    tl = np.tril(np.ones((64, 64), np.float32), -1)
    maskSL = f(np.concatenate([tl, tl], axis=0))
    tu = np.triu(np.ones((64, 64), np.float32), 1)
    maskSU = f(np.concatenate([tu, tu], axis=0))
    ui = np.triu(np.ones((64, 64), np.float32), 0)
    maskUI = f(np.concatenate([ui, ui], axis=0))
    e2 = np.eye(64, dtype=np.float32)
    eye2 = f(np.concatenate([e2, e2], axis=0))
    consts = np.zeros((128, NCONST), np.float32)
    consts[:, C_EPS6] = 1e-6
    consts[:, C_EPS5] = 1e-5
    consts[:, C_HALFPI] = np.pi / 2
    jm = np.zeros((128, 128), np.float32)
    for cch in range(64):
        jm[2 * cch + 1, 2 * cch] = 1.0
        jm[2 * cch, 2 * cch + 1] = -1.0
    ones128a = np.ones((128, 128), np.float32)
    expa = np.zeros((8, 16, 128), np.float32)
    for hp_ in range(8):
        expa[hp_, 2 * hp_, 0:64] = 1.0
        expa[hp_, 2 * hp_ + 1, 64:128] = 1.0
    expa = f(expa.transpose(1, 0, 2).reshape(16, 1024))

    shared = dict(
        wmod_t=wmod_t, cvec_fm=cvec_fm, condw_t=condw_t, theta_col=theta_col,
        cnw_col=cnw_col, deltaw_sl=deltaw_sl, ramp_fm=ramp_fm,
        bramp_col=bramp_col, ctrlw_t=ctrlw_t, ctrlb_fm=ctrlb_fm,
        wgate_tm=wgate_tm, gateb_row=gateb_row, convw=convw,
        outnw_col=outnw_col, wop_t=wop_t, opb_fm=opb_fm,
        wg_nat=wg_nat, wu_nat=wu_nat, wd_nat=wd_nat, clip2=clip2,
        ident=ident, identA=identA, identB=identB,
        bdones=bdones, maskSL=maskSL, maskSU=maskSU,
        maskUI=maskUI, eye2=eye2, consts=consts, jmask=jm, ones128=ones128a,
        expa=expa)

    in_maps = []
    for cid in range(NCORES):
        m = dict(shared)
        m["xg"] = f(x[cid * T:(cid + 1) * T])
        m["xh"] = (np.zeros((3, D), np.float32) if cid == 0
                   else f(x[cid * T - 3:cid * T]))
        m["hmask"] = np.full((128, 1), 0.0 if cid == 0 else 1.0, np.float32)
        pm = np.zeros((128, 8), np.float32)
        pm[:, :cid] = 1.0
        m["prevmask"] = pm
        m["prevmaskc"] = (1.0 - pm).astype(np.float32)
        oh = np.zeros((128, 8), np.float32)
        oh[:, cid] = 1.0
        m["onehot8"] = oh
        in_maps.append(m)
    return in_maps


def kernel(**inputs):
    in_maps = host_prep(inputs)
    nc, _ = build()
    res = run_bass_kernel_spmd(nc, in_maps, core_ids=list(range(NCORES)))
    outs = [r["out"] for r in res.results]
    return np.concatenate(outs, axis=0).reshape(1, 16384, D)
